# version 14
# baseline (speedup 1.0000x reference)
import contextlib
import numpy as np
import ml_dtypes
import concourse.bass as bass
import concourse.mybir as mybir
from concourse.bass_utils import run_bass_kernel_spmd

F32 = mybir.dt.float32
BF16 = mybir.dt.bfloat16
AF = mybir.ActivationFunctionType
ALU = mybir.AluOpType
AX = mybir.AxisListType
AP = bass.AP

NCORES = 8
NSEQ = 4
S = 2048
DM = 1024
NT = 16
NH = 16
NPAIR = 8
DH = 64
DC = 8
QC = 4
NEXP = 10
HE = 768
NJ = 6
EPS = 1e-5

ENGS = ('sync', 'gpsimd', 'scalar', 'vector', 'tensor')

ALIAS = {
    'wbufA': ['w1bufA', 'w1bufB', 'rbA1', 'rbB1'],
    'wbufB': ['w1bufB'],
    'w1bufA': ['wbufA', 'rbA1', 'rbB1'],
    'w1bufB': ['wbufA', 'wbufB'],
    'w2bufA': ['xs0', 'xs1', 'msc1', 'lnbf2'],
    'w2bufB': ['msc2', 'lnbf', 'expt0', 'expt1', 'expt2', 'expt3', 'rbA', 'rbB'],
    'xs0': ['w2bufA', 'lnbf2', 'lnbf2b'], 'xs1': ['w2bufA', 'qsbb', 'lnbf2c'], 'msc1': ['w2bufA', 'msc1b'],
    'lnbf2': ['w2bufA', 'xs0'], 'lnbf2b': ['w2bufA', 'xs0'], 'lnbf2c': ['w2bufA', 'xs1', 'qsbb'],
    'msc2': ['w2bufB', 'msc2b'], 'msc2b': ['w2bufB', 'msc2'], 'msc1b': ['w2bufA', 'msc1'], 'qsbb': ['w2bufA', 'xs1'], 'lnbf': ['w2bufB', 'rbB'],
    'expt0': ['w2bufB', 'gnat'], 'expt1': ['w2bufB', 'gnat'], 'expt2': ['w2bufB'], 'expt3': ['w2bufB'],
    'rbA': ['w2bufB'], 'rbB': ['lnbf', 'w2bufB'],
    'rbA1': ['wbufA', 'w1bufA'], 'rbB1': ['wbufA', 'w1bufA'],
    'gnat': ['expt0', 'expt1', 'w2bufB'],
    'q_fm': [f'res:{i}' for i in range(8)],
    'k_fm': [f'res:{i}' for i in range(8, 16)],
    'v_aug': [f'act:{j}' for j in range(NJ)] + ['wrep', 'w_fm', 'dumpb'],
    'dumpb': ['v_aug', 'act:0'],
    'wrep': ['v_aug'], 'w_fm': ['v_aug'],
}
for i in range(8):
    ALIAS[f'res:{i}'] = ['q_fm']
for i in range(8, 16):
    ALIAS[f'res:{i}'] = ['k_fm']
for j in range(NJ):
    ALIAS[f'act:{j}'] = ['v_aug']
ALIAS['act:0'].append('dumpb')


class Sem:
    def __init__(self, handle, name):
        self.h = handle
        self.name = name
        self.n = 0

    def tok(self, k=1):
        self.n += k
        return (self, self.n)


class Prog:
    def __init__(self):
        self.ops = {e: [] for e in ENGS}
        self.waited = {e: {} for e in ENGS}
        self.own = {}

    def em(self, eng, fn):
        self.ops[eng].append(fn)

    def wait(self, eng, tok, force=False):
        if tok is None:
            return
        sem, val = tok
        assert val <= sem.n, f"forward wait {sem.name}:{val}>{sem.n}"

        w = self.waited[eng]
        if w.get(sem.name, -1) >= val:
            return
        w[sem.name] = val
        self.em(eng, lambda e, h=sem.h, v=val: e.wait_ge(h, v))


class Deps:

    def __init__(self, prog):
        self.p = prog
        self.d = {}

    def _expand(self, keys):
        out = list(keys)
        for k in keys:
            out.extend(ALIAS.get(k, ()))
        return out

    def rd(self, eng, *keys):
        for k in keys:
            e = self.d.get(k)
            if e:
                for t in e['w']:
                    self.p.wait(eng, t)

    def wr(self, eng, *keys):
        for k in self._expand(keys):
            e = self.d.get(k)
            if e:
                for t in e['w']:
                    self.p.wait(eng, t)
                for t in e['r']:
                    self.p.wait(eng, t)

    def note_w(self, tok, *keys):
        for k in keys:
            e = self.d.setdefault(k, {'w': [], 'r': []})
            e['w'].append(tok)
            if len(e['w']) > 24:
                e['w'] = e['w'][-24:]
            e['r'] = []

    def note_r(self, tok, *keys):
        for k in keys:
            e = self.d.setdefault(k, {'w': [], 'r': []})
            e['r'].append(tok)


def build_program(nseq=NSEQ, debug=False, stop_after=None, ln1_tiles=NT, ln_steps=99, pj_steps=99, skip_casts=False, skip_tab=False, skip_gate=False):
    nc = bass.Bass("TRN2", target_bir_lowering=False)

    dx = nc.dram_tensor('x', [nseq, S, DM], F32, kind='ExternalInput')
    dwq = nc.dram_tensor('wq', [DM, DM], F32, kind='ExternalInput')
    dwk = nc.dram_tensor('wk', [DM, DM], F32, kind='ExternalInput')
    dwv = nc.dram_tensor('wv', [DM, DM], F32, kind='ExternalInput')
    dwo = nc.dram_tensor('wo', [DM, DM], F32, kind='ExternalInput')
    dbq = nc.dram_tensor('bq', [DM], F32, kind='ExternalInput')
    dbk = nc.dram_tensor('bk', [DM], F32, kind='ExternalInput')
    dbv = nc.dram_tensor('bv', [DM], F32, kind='ExternalInput')
    dbo = nc.dram_tensor('bo', [DM], F32, kind='ExternalInput')
    dgate = nc.dram_tensor('gate_w', [8, DM], F32, kind='ExternalInput')
    dw1 = nc.dram_tensor('w1', [8, 2 * HE, DM], F32, kind='ExternalInput')
    dw2 = nc.dram_tensor('w2', [8, DM, HE], F32, kind='ExternalInput')
    dws1 = nc.dram_tensor('ws1', [4 * HE, DM], F32, kind='ExternalInput')
    dws2 = nc.dram_tensor('ws2', [DM, 2 * HE], F32, kind='ExternalInput')
    dcos = nc.dram_tensor('cos_t', [S, DH], F32, kind='ExternalInput')
    dsin = nc.dram_tensor('sin_t', [S, DH], F32, kind='ExternalInput')
    dident_bf = nc.dram_tensor('ident_bf', [128, 128], BF16, kind='ExternalInput')
    dcausal = nc.dram_tensor('causal_bf', [128, 128], BF16, kind='ExternalInput')
    dsel = nc.dram_tensor('sel_bf', [8, 1024], BF16, kind='ExternalInput')
    dout = nc.dram_tensor('out', [nseq, S, DM], F32, kind='ExternalOutput')
    if debug:
        ddbg = {k: nc.dram_tensor(k, shp, F32, kind='ExternalOutput') for k, shp in [
            ('dbg_fm1', [128, DC, S]), ('dbg_q', [128, NPAIR, S]), ('dbg_k', [128, NPAIR, S]),
            ('dbg_v', [128, NT, NH * 65]), ('dbg_c', [128, 1]), ('dbg_attn', [128, DC, S]),
            ('dbg_res1', [128, NT, DM]), ('dbg_w', [128, NT, 8]), ('dbg_fm2', [128, DC, S]),
            ('dbg_act0', [128, NJ, S]), ('dbg_w1A', [128, DC, HE]), ('dbg_w1B', [128, DC, HE]), ('dbg_w2', [128, NJ, DM])]}

    swq = nc.dram_tensor('swq', [DM, DM], BF16, kind='Internal')
    swk = nc.dram_tensor('swk', [DM, DM], BF16, kind='Internal')
    swv = nc.dram_tensor('swv', [DM, DM], BF16, kind='Internal')
    swo = nc.dram_tensor('swo', [DM, DM], BF16, kind='Internal')
    sw1 = nc.dram_tensor('sw1', [8, 2 * HE, DM], BF16, kind='Internal')
    sw2 = nc.dram_tensor('sw2', [8, DM, HE], BF16, kind='Internal')
    sws1 = nc.dram_tensor('sws1', [4 * HE, DM], BF16, kind='Internal')
    sws2 = nc.dram_tensor('sws2', [DM, 2 * HE], BF16, kind='Internal')
    sgate = nc.dram_tensor('sgate', [8, DM], BF16, kind='Internal')

    BASE = (nc.sbuf_base + 63) // 64 * 64

    def sb(name, shape, dtype, at):
        return nc.alloc_sbuf_tensor_at(name, shape, dtype, offset=BASE + at)

    res = sb('res', [128, NT, DM], F32, 0)
    q_fm = sb('q_fm', [128, NPAIR, S], BF16, 0)
    k_fm = sb('k_fm', [128, NPAIR, S], BF16, 32768)
    fm = sb('fm', [128, DC, S], BF16, 65536)
    VB = 98304
    v_aug = sb('v_aug', [128, NT, NH * 65], BF16, VB)
    dumpb = sb('dumpb', [128, DM], BF16, VB)
    act = sb('act', [128, NJ, S], BF16, VB)
    wrep = sb('wrep', [128, S], BF16, VB + 24576)
    w_fm = sb('w_fm', [8, S], BF16, VB + 28672)
    WB = 131584
    wbufA = sb('wbufA', [128, DC, DM], BF16, WB)
    wbufB = sb('wbufB', [128, DC, DM], BF16, WB + 16384)
    rbA1 = sb('rbA1', [128, 512], F32, WB)
    rbB1 = sb('rbB1', [128, 512], F32, WB + 2048)
    w1bufA = sb('w1bufA', [128, DC, HE], BF16, WB)
    w1bufB = sb('w1bufB', [128, DC, HE], BF16, WB + 12288)
    PB = 164352
    xslot = [sb('xslot0', [128, DM], F32, PB),
             sb('xslot1', [128, DM], F32, PB + 4096)]
    msc1 = sb('msc1', [128, DM], F32, PB + 8192)
    msc2 = sb('msc2', [128, DM], F32, PB + 12288)
    msc1b = sb('msc1b', [128, DM], BF16, PB + 8192)
    msc2b = sb('msc2b', [128, DM], BF16, PB + 12288)
    qsbb = sb('qsbb', [128, DM], BF16, PB + 4096)
    lnbf = sb('lnbf', [128, DM], BF16, PB + 16384)
    lnbf2 = sb('lnbf2', [128, DM], BF16, PB)
    lnbf2b = sb('lnbf2b', [128, DM], BF16, PB + 2048)
    lnbf2c = sb('lnbf2c', [128, DM], BF16, PB + 6144)
    expt = [sb(f'expt{i}', [128, 512], BF16, PB + 18432 + 1024 * i) for i in range(4)]
    gnat = sb('gnat', [8, 1024], BF16, PB + 18432)
    rbB = sb('rbB', [128, 512], F32, PB + 16384)
    rbA = sb('rbA', [128, 512], F32, PB + 22528)
    w2bufA = sb('w2bufA', [128, NJ, DM], BF16, PB)
    w2bufB = sb('w2bufB', [128, NJ, DM], BF16, PB + 12288)
    off = PB + 24576
    biaspg = sb('biaspg', [1, 4 * DM], BF16, off); off += 8192
    cosr = sb('cosr', [128, NT, DH], BF16, off); off += 2048
    sinr = sb('sinr', [128, NT, DH], BF16, off); off += 2048
    ident_bf = sb('ident_bf', [128, 128], BF16, off); off += 256
    causal = sb('causal', [128, 128], BF16, off); off += 256
    sel = sb('sel', [8, 1024], BF16, off); off += 2048
    gateT = sb('gateT', [128, DC, 16], BF16, off); off += 512
    ones_bf = sb('ones_bf', [1, 128], BF16, off); off += 256
    ones_col = sb('ones_col', [128, 1], BF16, off); off += 32
    ones32 = sb('ones32', [128, 64], F32, off); off += 256
    stat_nm = sb('stat_nm', [128, 2], F32, off); off += 32
    stat_rs = sb('stat_rs', [128, 2], F32, off); off += 32
    stat_v1 = sb('stat_v1', [128, 4], F32, off); off += 32
    qn2 = sb('qn2', [128, NT], F32, off); off += 64
    qn2m = sb('qn2m', [128, 1], BF16, off); off += 32
    kn2m = sb('kn2m', [128, 1], BF16, off); off += 32
    cnst = sb('cnst', [1, 16], F32, off); off += 64
    negC8 = sb('negC8', [128, 1], F32, off); off += 32
    g_s = sb('g_s', [128, 32], F32, off); off += 128
    g_t1 = sb('g_t1', [128, 32], F32, off); off += 128
    g_m = sb('g_m', [128, 4], F32, off); off += 32
    g_r = sb('g_r', [128, 4], F32, off); off += 32
    w_tok = sb('w_tok', [128, NT, 8], BF16, off); off += 256
    assert BASE + off <= 229000, (BASE, off)
    nc.sbuf_base = BASE + off

    pq0 = nc.alloc_psum_tensor('pq0', [128, 1024], F32)
    pq1 = nc.alloc_psum_tensor('pq1', [128, 1024], F32)
    pv = nc.alloc_psum_tensor('pv', [128, 512], F32)
    pvo = nc.alloc_psum_tensor('pvo', [128, 512], F32)
    pt = nc.alloc_psum_tensor('pt', [128, 1024], BF16)
    prb = nc.alloc_psum_tensor('prb', [128, 512], F32)

    P = Prog()
    dep = Deps(P)
    stack = contextlib.ExitStack()

    def sem(name):
        return Sem(stack.enter_context(nc.semaphore(name)), name)

    s_scr = sem('s_scr')
    s_x = [sem('s_x0'), sem('s_x1')]
    s_out = sem('s_out')
    s_tab = sem('s_tab')
    s_wa = sem('s_wa')
    s_wb = sem('s_wb')
    s_w2a = sem('s_w2a')
    s_w2b = sem('s_w2b')
    s_dbg = sem('s_dbg')
    s_t = sem('s_t')
    s_v = sem('s_v')
    s_s = sem('s_s')
    s_gp = sem('s_gp')
    P.own = {'tensor': s_t, 'vector': s_v, 'scalar': s_s, 'gpsimd': s_gp}

    em, wait = P.em, P.wait

    def E(eng, fn, inc=None):
        if inc is None:
            em(eng, fn)
            return None
        t = inc.tok()
        em(eng, lambda e, fn=fn, h=inc.h: fn(e).then_inc(h, 1))
        return t

    def gp_dma(dst, src, semx):
        t = semx.tok(16)
        em('gpsimd', lambda e, d=dst, s=src, h=semx.h: e.dma_start(d, s).then_inc(h, 16))
        return t

    if not skip_casts:
        gp_dma(swq[:, :], dwq[:, :], s_scr)
        gp_dma(swk[:, :], dwk[:, :], s_scr)
        gp_dma(swv[:, :], dwv[:, :], s_scr)
        gp_dma(swo[:, :], dwo[:, :], s_scr)
        gp_dma(sw1[:, :, :], dw1[:, :, :], s_scr)
        gp_dma(sw2[:, :, :], dw2[:, :, :], s_scr)
        gp_dma(sws1[:, :], dws1[:, :], s_scr)
        gp_dma(sws2[:, :], dws2[:, :], s_scr)
    gp_dma(sgate[:, :], dgate[:, :], s_scr)
    gp_dma(biaspg[0:1, 0:DM], dbq.ap().rearrange('(a b) -> a b', a=1), s_scr)
    gp_dma(biaspg[0:1, DM:2 * DM], dbk.ap().rearrange('(a b) -> a b', a=1), s_scr)
    gp_dma(biaspg[0:1, 2 * DM:3 * DM], dbv.ap().rearrange('(a b) -> a b', a=1), s_scr)
    gp_dma(biaspg[0:1, 3 * DM:4 * DM], dbo.ap().rearrange('(a b) -> a b', a=1), s_scr)
    gp_dma(gnat[:, :], dgate[:, :], s_scr)
    gp_dma(cosr[:, :, :], AP(dcos, 0, [[DH, 128], [128 * DH, NT], [1, DH]]), s_scr)
    gp_dma(sinr[:, :, :], AP(dsin, 0, [[DH, 128], [128 * DH, NT], [1, DH]]), s_scr)
    scr_done = (s_scr, s_scr.n)
    dep.note_w(scr_done, 'biaspg', 'gnat')

    def sy_dma(dst, src, semx, transpose=False):
        t = semx.tok(16)
        em('sync', lambda e, d=dst, s=src, h=semx.h, tr=transpose:
           e.dma_start(d, s, transpose=tr).then_inc(h, 16))
        return t

    if not skip_tab:
        sy_dma(causal[:, :], dcausal[:, :], s_tab)
        sy_dma(sel[:, :], dsel[:, :], s_tab)
    sy_dma(ident_bf[:, :], dident_bf[:, :], s_tab)
    tab_done = (s_tab, s_tab.n)
    dep.note_w(tab_done, 'tables')
    dep.note_w(scr_done, 'tables')
    if not skip_gate:
        wait('tensor', scr_done)
        wait('tensor', tab_done)
        dep.wr('tensor', 'pt0', 'pt1')
        for c in range(DC):
            gt = E('tensor', lambda e, c=c: e.transpose(pt[:, 8 * c:8 * (c + 1)], gnat[0:8, 128 * c:128 * (c + 1)], ident_bf[0:8, 0:8]),
                   s_t if c == DC - 1 else None)
        dep.note_w(gt, 'pt0')
        dep.note_r(gt, 'gnat')
        wait('vector', gt)
        gte = E('vector', lambda e: e.tensor_copy(
            AP(gateT, 0, [[DC * 16, 128], [16, DC], [1, 8]]),
            pt[:, 0:64].rearrange('p (a b) -> p a b', b=8)), s_v)
        dep.note_w(gte, 'gateT')
        dep.note_r(gte, 'pt0')

    E('vector', lambda e: e.memset(ones_bf[:, :], 1.0))
    E('vector', lambda e: e.memset(ones_col[:, :], 1.0))
    t_ones = E('vector', lambda e: e.memset(ones32[:, :], 1.0), s_v)
    dep.note_w(t_ones, 'ones')

    def fill(dst_sb, key, src_slices, semx):
        dep.wr('sync', key)
        wait('sync', scr_done)
        for c, sap in enumerate(src_slices):
            n = sap.shape[0]
            sy_dma(dst_sb[:, c, 0:n], sap, semx, transpose=True)
        tok = (semx, semx.n)
        dep.note_w(tok, key)
        return tok

    def qkv_slices(scr):
        return [scr[0:DM, 128 * c:128 * (c + 1)] for c in range(DC)]

    def w1_slices(eidx, half):
        out = []
        for c in range(DC):
            if eidx < 8:
                out.append(sw1[eidx, half * HE:(half + 1) * HE, 128 * c:128 * (c + 1)])
            else:
                r0 = half * 2 * HE + (eidx - 8) * HE
                out.append(sws1[r0:r0 + HE, 128 * c:128 * (c + 1)])
        return out

    def w2_slices(eidx):
        out = []
        for j in range(NJ):
            if eidx < 8:
                out.append(sw2[eidx, 0:DM, 128 * j:128 * (j + 1)])
            else:
                c0 = (eidx - 8) * HE + 128 * j
                out.append(sws2[0:DM, c0:c0 + 128])
        return out

    def x_load(sq, i, dst, dkey):
        par = i % 2
        sx = s_x[par]
        dep.wr('gpsimd', dkey)
        wait('gpsimd', (sx, sx.n) if sx.n else None)
        t = sx.tok(16)
        em('gpsimd', lambda e, sq=sq, i=i, d=dst, h=sx.h:
           e.dma_start(d, dx[sq, 128 * i:128 * (i + 1), :]).then_inc(h, 16))
        dep.note_w(t, dkey)
        return t

    def transpose_to(dst, dsti_off, src_tile, key_out, no_evac=False, src_key='lnbf'):
        dep.rd('tensor', 'tables')
        dep.wr('tensor', 'pt0', 'pt1')
        for cc in range(8):
            fn = lambda e, cc=cc, st=src_tile: e.transpose(
                pt[:, 128 * cc:128 * (cc + 1)],
                st[:, 128 * cc:128 * (cc + 1)], ident_bf[:, :])
            tk = E('tensor', fn, s_t if cc == 7 else None)
        dep.note_w(tk, 'pt0', 'pt1')
        dep.note_r(tk, src_key)
        if no_evac:
            return tk
        wait('vector', tk)
        dep.wr('vector', key_out)
        ev = E('vector', lambda e, o=dsti_off, d=dst: e.tensor_copy(
            AP(d, o, [[DC * S, 128], [S, DC], [1, 128]]),
            pt[:, :].rearrange('p (a b) -> p a b', b=128)), s_v)
        dep.note_r(ev, 'pt0', 'pt1')
        dep.note_w(ev, key_out)
        return ev

    def ln_A(i, src, sk, par):
        stn = stat_nm[:, par:par + 1]
        mbuf = msc1 if par == 0 else msc2
        mk = 'msc1' if par == 0 else 'msc2'
        dep.rd('vector', *sk)
        dep.wr('vector', f'stat_nm{par}')
        t = E('vector', lambda e, s=src, stn=stn: e.reduce_sum(stn, s, axis=AX.X, negate=True), s_v)
        P.wait('vector', t, force=True)
        nm = E('vector', lambda e, stn=stn: e.tensor_scalar_mul(stn, stn, 1.0 / DM), s_v)
        dep.note_w(nm, f'stat_nm{par}')
        P.wait('vector', nm, force=True)
        dep.wr('vector', mk)
        xm = E('vector', lambda e, s=src, stn=stn, mbuf=mbuf: e.tensor_scalar_add(mbuf[:, :], s, stn), s_v)
        dep.note_w(xm, mk)
        dep.note_r(xm, *sk, f'stat_nm{par}')
        return xm, mbuf, mk

    def ln_S(i, par, xm, mbuf, mk):
        wait('scalar', xm)
        dep.wr('scalar', 'dumpb', f'stat_v1{par}')
        vv = E('scalar', lambda e, mbuf=mbuf, par=par: e.activation(
            dumpb[:, :], mbuf[:, :], AF.Square, accum_out=stat_v1[:, par:par + 1]), s_s)
        dep.note_w(vv, 'dumpb', f'stat_v1{par}')
        dep.note_r(vv, mk)
        return vv

    def ln_B(i, par, vv, mbuf, mk, key_out):
        srs = stat_rs[:, par:par + 1]
        sv1 = stat_v1[:, par:par + 1]
        lnb = lnbf if par == 0 else lnbf2
        lbk = 'lnbf' if par == 0 else 'lnbf2'
        wait('vector', vv)
        dep.wr('vector', f'stat_rs{par}', f'stat_v1{par}')
        t = E('vector', lambda e, sv1=sv1: e.tensor_scalar(sv1, sv1, 1.0 / DM, EPS, ALU.mult, ALU.add), s_v)
        P.wait('vector', t, force=True)
        r1 = E('vector', lambda e, srs=srs, sv1=sv1: e.reciprocal(srs, sv1), s_v)
        dep.note_w(r1, f'stat_v1{par}')
        wait('scalar', r1)
        rs = E('scalar', lambda e, srs=srs: e.activation(srs, srs, AF.Sqrt), s_s)
        dep.note_w(rs, f'stat_rs{par}')
        wait('vector', rs)
        dep.wr('vector', lbk)
        ln = E('vector', lambda e, mbuf=mbuf, srs=srs, lnb=lnb: e.tensor_scalar_mul(lnb[:, :], mbuf[:, :], srs), s_v)
        dep.note_w(ln, lbk)
        dep.note_r(ln, mk, f'stat_rs{par}')
        wait('tensor', ln)
        transpose_to(fm, 128 * i, lnb, key_out, src_key=lbk)

    def ln_pass(sq, src_fn, src_key_fn, with_load, ntiles=NT):
        pend = None
        for i in range(ntiles):
            if with_load:
                if i == 0:
                    x_load(sq, 0, xslot[0][:, :], 'xs0')
                    x_load(sq, 1, xslot[1][:, :], 'xs1')
                elif i + 1 < ntiles:
                    x_load(sq, i + 1, xslot[(i + 1) % 2][:, :], f'xs{(i + 1) % 2}')
            par = i % 2
            xm, mbuf, mk = ln_A(i, src_fn(i), src_key_fn(i), par)
            vv = ln_S(i, par, xm, mbuf, mk)
            if pend is not None:
                ln_B(*pend)
            pend = (i, par, vv, mbuf, mk, f'fmc:{i // 4}')
        ln_B(*pend)

    def rope_chain(i, pq, pks, kind, qbuf, qbk):
        cos_ap = AP(cosr, i * DH, [[NT * DH, 128], [0, NH], [1, DH]])
        sin_ap = AP(sinr, i * DH, [[NT * DH, 128], [0, NH], [1, DH]])
        dep.rd('scalar', *pks)
        dep.wr('scalar', 'qsbb')
        qv = E('scalar', lambda e, pq=pq: e.copy(qsbb[:, :], pq[:, :]), s_s)
        dep.note_w(qv, 'qsbb')
        dep.note_r(qv, *pks)
        wait('vector', qv)
        dep.rd('vector', 'tables')
        dep.wr('vector', 'msc1b')
        t1 = E('vector', lambda e, ca=cos_ap: e.tensor_tensor(
            msc1b[:, :].rearrange('p (h d) -> p h d', d=DH),
            qsbb[:, :].rearrange('p (h d) -> p h d', d=DH), ca, ALU.mult), s_v)
        dep.note_w(t1, 'msc1b')
        dep.wr('scalar', 'msc2b')
        P.wait('scalar', qv, force=True)
        E('scalar', lambda e: e.activation(
            AP(msc2b, 0, [[DM, 128], [DH, NH], [1, 32]]),
            AP(qsbb, 32, [[DM, 128], [DH, NH], [1, 32]]), AF.Copy, scale=-1.0))
        rot = E('scalar', lambda e: e.activation(
            AP(msc2b, 32, [[DM, 128], [DH, NH], [1, 32]]),
            AP(qsbb, 0, [[DM, 128], [DH, NH], [1, 32]]), AF.Copy), s_s)
        dep.note_w(rot, 'msc2b')
        dep.note_r(rot, 'qsbb')
        dep.note_r(t1, 'qsbb')
        wait('vector', rot)
        sm = E('vector', lambda e, sa=sin_ap: e.tensor_tensor(
            msc2b[:, :].rearrange('p (h d) -> p h d', d=DH),
            msc2b[:, :].rearrange('p (h d) -> p h d', d=DH), sa, ALU.mult), s_v)
        dep.wr('vector', qbk)
        P.wait('vector', t1, force=True)
        P.wait('vector', sm, force=True)
        qr = E('vector', lambda e, qbuf=qbuf: e.tensor_tensor(qbuf[:, :], msc1b[:, :], msc2b[:, :], ALU.add), s_v)
        dep.note_w(qr, qbk)
        dep.note_r(qr, 'msc1b', 'msc2b')
        acc = qn2m if kind == 'q' else kn2m
        acck = 'qn2m' if kind == 'q' else 'kn2m'
        dep.wr('vector', 'msc1b', 'qn2', acck)
        P.wait('vector', qr, force=True)
        t = E('vector', lambda e, qbuf=qbuf: e.tensor_tensor(msc1b[:, :], qbuf[:, :], qbuf[:, :], ALU.mult), s_v)
        P.wait('vector', t, force=True)
        t = E('vector', lambda e: e.reduce_sum(
            qn2[:, :], msc1b[:, :].rearrange('p (h d) -> p h d', d=DH), axis=AX.X), s_v)
        if i == 0:
            E('vector', lambda e, acc=acc: e.memset(acc[:, :], 0.0))
        P.wait('vector', t, force=True)
        t = E('vector', lambda e: e.reduce_max(stat_v1[:, 0:1], qn2[:, :], axis=AX.X), s_v)
        P.wait('vector', t, force=True)
        n2 = E('vector', lambda e, acc=acc: e.tensor_tensor(acc[:, :], acc[:, :], stat_v1[:, 0:1], ALU.max), s_v)
        dep.note_w(n2, acck, 'msc1b', 'stat_v1')
        return qr

    def sweep_qk(sq):
        pend = [None]
        QB = [(lnbf, 'lnbf'), (lnbf2, 'lnbf2')]
        KB = [(lnbf2b, 'lnbf2b'), (lnbf2c, 'lnbf2c')]
        for i in range(NT):
            par = i % 2
            dep.rd('tensor', f'fmc:{i // 4}', 'wbufA', 'wbufB', 'biaspg')
            dep.wr('tensor', 'pq0lo', 'pq0hi', 'pq1lo', 'pq1hi')
            for c in range(DC):
                for wb, pq in ((wbufA, pq0), (wbufB, pq1)):
                    for oh in range(2):
                        E('tensor', lambda e, i=i, c=c, oh=oh, pq=pq, wb=wb:
                          e.matmul(pq[:, 512 * oh:512 * (oh + 1)],
                                   fm[:, c, 128 * i:128 * (i + 1)],
                                   wb[:, c, 512 * oh:512 * (oh + 1)],
                                   start=(c == 0), stop=False))
            for boff, pq in ((0, pq0), (DM, pq1)):
                for oh in range(2):
                    tk = E('tensor', lambda e, oh=oh, pq=pq, boff=boff:
                           e.matmul(pq[:, 512 * oh:512 * (oh + 1)], ones_bf[0:1, :],
                                    biaspg[0:1, boff + 512 * oh:boff + 512 * (oh + 1)],
                                    start=False, stop=True),
                           s_t if (oh == 1 and boff == DM) else None)
            dep.note_w(tk, 'pq0lo', 'pq0hi', 'pq1lo', 'pq1hi')
            dep.note_r(tk, f'fmc:{i // 4}', 'wbufA', 'wbufB')
            qrq = rope_chain(i, pq0, ('pq0lo', 'pq0hi'), 'q', *QB[par])
            qrk = rope_chain(i, pq1, ('pq1lo', 'pq1hi'), 'k', *KB[par])

            def do_tp(i=i, qrq=qrq, qrk=qrk, par=par):
                wait('tensor', qrq)
                if i == 0:
                    dep.wr('tensor', 'q_fm', 'k_fm')
                transpose_to(q_fm, 128 * i, QB[par][0], 'q_fm', src_key=QB[par][1])
                wait('tensor', qrk)
                transpose_to(k_fm, 128 * i, KB[par][0], 'k_fm', src_key=KB[par][1])
            if pend[0] is not None:
                pend[0]()
            pend[0] = do_tp
        pend[0]()

    def proj_sweep(sq, kind, wbuf, wkey):
        boff = {'q': 0, 'k': DM, 'v': 2 * DM}[kind]
        pend_tp = [None]
        for i in range(NT):
            par = i % 2
            pq = pq0 if par == 0 else pq1
            pks = (f'pq{par}lo', f'pq{par}hi')
            dep.rd('tensor', f'fmc:{i // 4}', wkey, 'biaspg')
            dep.wr('tensor', pks[0], pks[1])
            for c in range(DC):
                for oh in range(2):
                    E('tensor', lambda e, i=i, c=c, oh=oh, pq=pq, wb=wbuf:
                      e.matmul(pq[:, 512 * oh:512 * (oh + 1)],
                               fm[:, c, 128 * i:128 * (i + 1)],
                               wb[:, c, 512 * oh:512 * (oh + 1)],
                               start=(c == 0), stop=False))
            for oh in range(2):
                tk = E('tensor', lambda e, oh=oh, pq=pq, boff=boff:
                       e.matmul(pq[:, 512 * oh:512 * (oh + 1)], ones_bf[0:1, :],
                                biaspg[0:1, boff + 512 * oh:boff + 512 * (oh + 1)],
                                start=False, stop=True), s_t if oh == 1 else None)
            mm = tk
            dep.note_w(mm, pks[0], pks[1])
            dep.note_r(mm, f'fmc:{i // 4}', wkey)
            if pj_steps < 2:
                continue
            if kind in ('q', 'k'):
                cos_ap = AP(cosr, i * DH, [[NT * DH, 128], [0, NH], [1, DH]])
                sin_ap = AP(sinr, i * DH, [[NT * DH, 128], [0, NH], [1, DH]])
                wait('scalar', mm)
                dep.wr('scalar', 'qsbb')
                qv = E('scalar', lambda e, pq=pq: e.copy(qsbb[:, :], pq[:, :]), s_s)
                dep.note_w(qv, 'qsbb')
                dep.note_r(qv, *pks)
                wait('vector', qv)
                dep.rd('vector', 'tables')
                dep.wr('vector', 'msc1b')
                t1 = E('vector', lambda e, ca=cos_ap: e.tensor_tensor(
                    msc1b[:, :].rearrange('p (h d) -> p h d', d=DH),
                    qsbb[:, :].rearrange('p (h d) -> p h d', d=DH), ca, ALU.mult), s_v)
                dep.note_w(t1, 'msc1b')
                dep.wr('scalar', 'msc2b')
                P.wait('scalar', qv, force=True)
                E('scalar', lambda e: e.activation(
                    AP(msc2b, 0, [[DM, 128], [DH, NH], [1, 32]]),
                    AP(qsbb, 32, [[DM, 128], [DH, NH], [1, 32]]), AF.Copy, scale=-1.0))
                rot = E('scalar', lambda e: e.activation(
                    AP(msc2b, 32, [[DM, 128], [DH, NH], [1, 32]]),
                    AP(qsbb, 0, [[DM, 128], [DH, NH], [1, 32]]), AF.Copy), s_s)
                dep.note_w(rot, 'msc2b')
                dep.note_r(rot, 'qsbb')
                dep.note_r(t1, 'qsbb')
                wait('vector', rot)
                sm = E('vector', lambda e, sa=sin_ap: e.tensor_tensor(
                    msc2b[:, :].rearrange('p (h d) -> p h d', d=DH),
                    msc2b[:, :].rearrange('p (h d) -> p h d', d=DH), sa, ALU.mult), s_v)
                qbuf, qbk = (lnbf, 'lnbf') if i % 2 == 0 else (lnbf2, 'lnbf2')
                dep.wr('vector', qbk)
                P.wait('vector', t1, force=True)
                P.wait('vector', sm, force=True)
                qr = E('vector', lambda e, qbuf=qbuf: e.tensor_tensor(qbuf[:, :], msc1b[:, :], msc2b[:, :], ALU.add), s_v)
                dep.note_w(qr, qbk)
                dep.note_r(qr, 'msc1b', 'msc2b')
                if pj_steps < 3:
                    continue
                acc = qn2m if kind == 'q' else kn2m
                acck = 'qn2m' if kind == 'q' else 'kn2m'
                dep.wr('vector', 'msc1b', 'qn2', acck)
                P.wait('vector', qr, force=True)
                t = E('vector', lambda e, qbuf=qbuf: e.tensor_tensor(msc1b[:, :], qbuf[:, :], qbuf[:, :], ALU.mult), s_v)
                P.wait('vector', t, force=True)
                t = E('vector', lambda e: e.reduce_sum(
                    qn2[:, :], msc1b[:, :].rearrange('p (h d) -> p h d', d=DH), axis=AX.X), s_v)
                if i == 0:
                    E('vector', lambda e, acc=acc: e.memset(acc[:, :], 0.0))
                P.wait('vector', t, force=True)
                t = E('vector', lambda e: e.reduce_max(stat_v1[:, 0:1], qn2[:, :], axis=AX.X), s_v)
                P.wait('vector', t, force=True)
                n2 = E('vector', lambda e, acc=acc: e.tensor_tensor(acc[:, :], acc[:, :], stat_v1[:, 0:1], ALU.max), s_v)
                dep.note_w(n2, acck, 'msc1b', 'stat_v1')
                if pj_steps < 4:
                    continue
                dst = q_fm if kind == 'q' else k_fm
                dkey = 'q_fm' if kind == 'q' else 'k_fm'

                def do_tp(i=i, qr=qr, qbuf=qbuf, qbk=qbk, dst=dst, dkey=dkey):
                    wait('tensor', qr)
                    if i == 0:
                        dep.wr('tensor', dkey)
                    transpose_to(dst, 128 * i, qbuf, dkey, src_key=qbk)
                if pend_tp[0] is not None:
                    pend_tp[0]()
                pend_tp[0] = do_tp
            else:
                if pend_tp[0] is not None:
                    pend_tp[0]()
                    pend_tp[0] = None
                wait('scalar', mm)
                if i == 0:
                    dep.wr('scalar', 'v_aug')
                va = E('scalar', lambda e, i=i, pq=pq: e.activation(
                    AP(v_aug, i * (NH * 65), [[NT * NH * 65, 128], [65, NH], [1, 64]]),
                    pq[:, :].rearrange('p (h d) -> p h d', d=DH), AF.Copy), s_s)
                dep.note_w(va, 'v_aug')
                dep.note_r(va, *pks)
                wait('vector', va)
                vo = E('vector', lambda e, i=i: e.memset(
                    AP(v_aug, i * (NH * 65) + 64, [[NT * NH * 65, 128], [65, NH], [1, 1]]), 1.0), s_v)
                dep.note_w(vo, 'v_aug')
        if pend_tp[0] is not None:
            pend_tp[0]()
            pend_tp[0] = None

    def dump(name, buf, *keys):
        if not debug:
            return
        for k in keys:
            dep.rd('gpsimd', k)
        if s_dbg.n:
            wait('gpsimd', (s_dbg, s_dbg.n))
        t = s_dbg.tok(16)
        em('gpsimd', lambda e, d=ddbg[name], b=buf, h=s_dbg.h:
           e.dma_start(d.ap(), b).then_inc(h, 16))
        for k in keys:
            dep.note_r(t, k)

    _stopped = [False]

    def phase_gate(name):
        if _stopped[0]:
            return False
        if stop_after is not None and name == stop_after:
            _stopped[0] = True
        return True

    fill(wbufA, 'wbufA', qkv_slices(swq), s_wa)
    for sq in range(nseq):
        if not phase_gate('prelude'):
            break
        fill(wbufB, 'wbufB', qkv_slices(swk), s_wb)
        if phase_gate('ln1'):
            ln_pass(sq, lambda i: xslot[i % 2][:, :], lambda i: (f'xs{i % 2}',), True, ntiles=ln1_tiles)
        if sq == 0:
            dump('dbg_fm1', fm[:, :, :], *[f'fmc:{c}' for c in range(4)])
        if phase_gate('q') and phase_gate('k'):
            sweep_qk(sq)
            fill(wbufA, 'wbufA', qkv_slices(swv), s_wa)
        if phase_gate('v'):
            proj_sweep(sq, 'v', wbufA, 'wbufA')
            fill(wbufB, 'wbufB', qkv_slices(swo), s_wb)

        if sq == 0:
            dump('dbg_q', q_fm[:, :, :], 'q_fm')
            dump('dbg_k', k_fm[:, :, :], 'k_fm')
            dump('dbg_v', v_aug[:, :, :], 'v_aug')
        if not phase_gate('negc'):
            continue
        dep.rd('tensor', 'qn2m', 'kn2m', 'tables')
        dep.wr('tensor', 'pt0')
        E('tensor', lambda e: e.transpose(pt[0:1, 0:128], qn2m[:, :], ident_bf[:, :]))
        tC = E('tensor', lambda e: e.transpose(pt[0:1, 128:256], kn2m[:, :], ident_bf[:, :]), s_t)
        dep.note_w(tC, 'pt0')
        dep.note_r(tC, 'qn2m', 'kn2m')
        wait('vector', tC)
        dep.wr('vector', 'cnst')
        E('vector', lambda e: e.reduce_max(cnst[0:1, 0:1], pt[0:1, 0:128], axis=AX.X))
        t = E('vector', lambda e: e.reduce_max(cnst[0:1, 1:2], pt[0:1, 128:256], axis=AX.X), s_v)
        P.wait('vector', t, force=True)
        c3 = E('vector', lambda e: e.tensor_tensor(cnst[0:1, 2:3], cnst[0:1, 0:1], cnst[0:1, 1:2], ALU.mult), s_v)
        dep.note_r(c3, 'pt0')
        wait('scalar', c3)
        c4 = E('scalar', lambda e: e.activation(cnst[0:1, 3:4], cnst[0:1, 2:3], AF.Sqrt, scale=1.0 / 64.0), s_s)
        wait('vector', c4)
        c5 = E('vector', lambda e: e.tensor_scalar_mul(cnst[0:1, 4:5], cnst[0:1, 3:4], -1.0), s_v)
        dep.note_w(c5, 'cnst')
        wait('tensor', c5)
        dep.rd('tensor', 'ones')
        dep.wr('tensor', 'prb')
        E('tensor', lambda e: e.matmul(prb[0:64, 0:1], ones32[0:1, :], cnst[0:1, 4:5], start=True, stop=True))
        c6 = E('tensor', lambda e: e.matmul(prb[64:128, 1:2], ones32[0:1, :], cnst[0:1, 4:5], start=True, stop=True), s_t)
        dep.note_w(c6, 'prb')
        dep.note_r(c6, 'cnst', 'ones')
        wait('scalar', c6)
        dep.wr('scalar', 'negC8')
        E('scalar', lambda e: e.copy(negC8[0:64, :], prb[0:64, 0:1]))
        c7 = E('scalar', lambda e: e.copy(negC8[64:128, :], prb[64:128, 1:2]), s_s)
        dep.note_w(c7, 'negC8')
        dep.note_r(c7, 'prb')

        if sq == 0:
            dump('dbg_c', negC8[:, :], 'negC8')
        if not phase_gate('attn'):
            continue
        pcs = [(p, c) for p in range(NPAIR) for c in range(QC)]

        def av_banks(k):
            if k % 2 == 0:
                return pv, 'pv', pvo, 'pvo'
            return pq1[:, 0:512], 'pq1lo', pq1[:, 512:1024], 'pq1hi'

        def jloop(k):
            p, c = pcs[k]
            avA, avAk, avB, avBk = av_banks(k)
            jmax = 4 * c + 3
            info = {}

            def sc_mm(j):
                o = max(0, 128 * j - 512 * c)
                wdt = 512 - o
                for half in range(2):
                    h0 = 64 * half
                    pk = 'pq0lo' if half == 0 else 'pq0hi'
                    scoff = 512 * half
                    dep.rd('tensor', 'q_fm', 'k_fm')
                    dep.wr('tensor', pk)
                    sct = E('tensor', lambda e, p=p, c=c, j=j, h0=h0, o=o, wdt=wdt, scoff=scoff:
                            e.matmul(pq0[:, scoff + o:scoff + o + wdt],
                                     k_fm[h0:h0 + 64, p, 128 * j:128 * (j + 1)],
                                     q_fm[h0:h0 + 64, p, 512 * c + o:512 * c + o + wdt],
                                     start=True, stop=True), s_t)
                    dep.note_w(sct, pk)
                    info[(j, half)] = [sct, o, wdt]

            def exp_j(j):
                for half in range(2):
                    sct, o, wdt = info[(j, half)]
                    pk = 'pq0lo' if half == 0 else 'pq0hi'
                    scoff = 512 * half
                    eb = expt[2 * half + (j % 2)]
                    ebk = f'expt{2 * half + (j % 2)}'
                    wait('scalar', sct)
                    P.wait('scalar', c7, force=True)
                    dep.rd('scalar', 'negC8')
                    dep.wr('scalar', ebk)
                    ex = E('scalar', lambda e, eb=eb, scoff=scoff, o=o, wdt=wdt:
                           e.activation(eb[:, o:o + wdt], pq0[:, scoff + o:scoff + o + wdt],
                                        AF.Exp, bias=negC8[:, 0:1], scale=0.125), s_s)
                    dep.note_w(ex, ebk)
                    dep.note_r(ex, pk, 'negC8')
                    if j >= 4 * c:
                        wait('gpsimd', ex)
                        dep.rd('gpsimd', 'tables')
                        dep.wr('gpsimd', ebk)
                        ex = E('gpsimd', lambda e, eb=eb, o=o: e.tensor_tensor(
                            eb[:, o:o + 128], eb[:, o:o + 128], causal[:, :], ALU.mult), s_gp)
                        dep.note_w(ex, ebk)
                    info[(j, half)] = (ex, eb, ebk, o, wdt)

            def av(j):
                for half in range(2):
                    ex, eb, ebk, o, wdt = info[(j, half)]
                    ebs = eb[:, o:o + wdt]
                    hh = 2 * p + half
                    wait('tensor', ex)
                    dep.rd('tensor', 'v_aug')
                    dep.wr('tensor', avAk if half == 0 else avBk)
                    if half == 0:
                        tk = E('tensor', lambda e, j=j, hh=hh, ebs=ebs, o=o, wdt=wdt, st=(j == 0), sp=(j == jmax), avA=avA:
                               e.matmul(avA[0:65, o:o + wdt], v_aug[:, j, 65 * hh:65 * hh + 65],
                                        ebs, start=st, stop=sp), s_t)
                    else:
                        E('tensor', lambda e, j=j, ebs=ebs, o=o, wdt=wdt, st=(j == 0), sp=(j == jmax), avB=avB:
                          e.matmul(avB[0:1, o:o + wdt], ones_col[:, 0:1],
                                   ebs, start=st, stop=sp))
                        tk = E('tensor', lambda e, j=j, hh=hh, ebs=ebs, o=o, wdt=wdt, st=(j == 0), sp=(j == jmax), avB=avB:
                               e.matmul(avB[64:128, o:o + wdt], v_aug[:, j, 65 * hh:65 * hh + 64],
                                        ebs, start=st, stop=sp), s_t)
                    dep.note_w(tk, avAk if half == 0 else avBk)
                    dep.note_r(tk, ebk)

            for j in range(jmax + 1):
                sc_mm(j)
                if j >= 2:
                    av(j - 2)
                exp_j(j)
            if jmax >= 1:
                av(jmax - 1)
            av(jmax)
            av_done = (s_t, s_t.n)
            ra, rak = (rbA, 'rbA') if k % 2 == 0 else (rbA1, 'rbA1')
            rb_, rbk = (rbB, 'rbB') if k % 2 == 0 else (rbB1, 'rbB1')
            wait('vector', av_done)
            dep.wr('vector', rak, rbk)
            E('vector', lambda e, avA=avA, ra=ra: e.reciprocal(ra[64:65, :], avA[64:65, :]))
            rc = E('vector', lambda e, avB=avB, rb_=rb_: e.reciprocal(rb_[0:1, :], avB[0:1, :]), s_v)
            dep.note_w(rc, rak, rbk)
            dep.note_r(rc, avAk, avBk)
            return (p, c, avA, avAk, avB, avBk, rc, ra, rak, rb_, rbk)

        def norm(st8):
            p, c, avA, avAk, avB, avBk, rc, ra, rak, rb_, rbk = st8
            wait('tensor', rc)
            dep.rd('tensor', 'ones')
            dep.wr('tensor', 'prb')
            E('tensor', lambda e, ra=ra: e.matmul(prb[0:64, :], ones32[64:65, :], ra[64:65, :], start=True, stop=True))
            rb = E('tensor', lambda e, rb_=rb_: e.matmul(prb[64:128, :], ones32[0:1, :], rb_[0:1, :], start=True, stop=True), s_t)
            dep.note_w(rb, 'prb')
            dep.note_r(rb, rak, rbk)
            wait('scalar', rb)
            dep.wr('scalar', rak, rbk)
            E('scalar', lambda e, ra=ra: e.copy(ra[0:64, :], prb[0:64, :]))
            rbe = E('scalar', lambda e, rb_=rb_: e.copy(rb_[64:128, :], prb[64:128, :]), s_s)
            dep.note_w(rbe, rak, rbk)
            dep.note_r(rbe, 'prb')
            wait('vector', rbe)
            dep.wr('vector', f'fmc:{c}')
            E('vector', lambda e, p=p, c=c, avA=avA, ra=ra: e.tensor_tensor(
                fm[0:64, p, 512 * c:512 * (c + 1)], avA[0:64, :], ra[0:64, :], ALU.mult))
            fme = E('vector', lambda e, p=p, c=c, avB=avB, rb_=rb_: e.tensor_tensor(
                fm[64:128, p, 512 * c:512 * (c + 1)], avB[64:128, :], rb_[64:128, :], ALU.mult), s_v)
            dep.note_w(fme, f'fmc:{c}')
            dep.note_r(fme, avAk, avBk, rak, rbk)

        pend = None
        for k in range(len(pcs)):
            st8 = jloop(k)
            if pend is not None:
                norm(pend)
            pend = st8
        norm(pend)
        phase2_done = (s_t, s_t.n)
        dep.note_r(phase2_done, 'v_aug', 'q_fm', 'k_fm')
        dep.note_r((s_v, s_v.n), 'q_fm', 'k_fm')

        if sq == 0:
            dump('dbg_attn', fm[:, :, :], *[f'fmc:{c}' for c in range(4)])
        if not phase_gate('wo'):
            continue
        phase_gate('ln2')
        phase_gate('gate')

        def gate_mms(i):
            sl = 8 * (i % 4)
            dep.rd('tensor', f'fmc:{i // 4}', 'tables', 'gateT')
            dep.wr('tensor', 'prb')
            for c in range(DC):
                tk = E('tensor', lambda e, i=i, c=c, sl=sl:
                       e.matmul(prb[:, sl:sl + 8], fm[:, c, 128 * i:128 * (i + 1)],
                                gateT[:, c, 0:8], start=(c == 0), stop=(c == DC - 1)),
                       s_t if c == DC - 1 else None)
            dep.note_w(tk, 'prb')
            dep.note_r(tk, f'fmc:{i // 4}')
            return tk

        def gate_chain(g, tk):
            L = prb[:, 0:32].rearrange('p (a b) -> p a b', b=8)
            mrep = AP(g_m, 0, [[4, 128], [1, 4], [0, 8]])
            rrep = AP(g_r, 0, [[4, 128], [1, 4], [0, 8]])
            a3 = g_s[:, :].rearrange('p (a b) -> p a b', b=8)
            b3 = g_t1[:, :].rearrange('p (a b) -> p a b', b=8)
            wait('vector', tk)
            dep.wr('vector', 'g_m', 'g_s', 'g_t1', 'g_r')
            ch = []

            def step(eng, fn):
                for tt in ch[-1:]:
                    P.wait(eng, tt, force=True)
                ch.append(E(eng, fn, s_v if eng == 'vector' else s_s))
            step('vector', lambda e: e.reduce_max(g_m[:, :], L, axis=AX.X, negate=True))
            step('vector', lambda e: e.tensor_tensor(a3, L, mrep, ALU.add))
            step('scalar', lambda e: e.activation(g_t1[:, :], g_s[:, :], AF.Exp))
            step('vector', lambda e: e.reduce_sum(g_r[:, :], b3, axis=AX.X))
            step('vector', lambda e: e.reciprocal(g_r[:, :], g_r[:, :]))
            step('vector', lambda e: e.tensor_tensor(a3, b3, rrep, ALU.mult))
            step('vector', lambda e: e.reduce_max(g_m[:, :], a3, axis=AX.X))
            step('vector', lambda e: e.tensor_tensor(b3, a3, mrep, ALU.is_ge))
            step('vector', lambda e: e.tensor_scalar(g_t1[:, :], g_t1[:, :], -1.0, 1.0, ALU.mult, ALU.add))
            step('vector', lambda e: e.tensor_tensor(b3, a3, b3, ALU.mult))
            step('vector', lambda e: e.reduce_max(g_m[:, :], b3, axis=AX.X))
            step('vector', lambda e: e.tensor_tensor(b3, a3, mrep, ALU.is_ge))
            ks = [f'w_tok:{4 * g + t}' for t in range(4)]
            dep.wr('vector', *ks)
            step('vector', lambda e, g=g: e.tensor_tensor(
                w_tok[:, 4 * g:4 * (g + 1), :], a3, b3, ALU.mult))
            wt = ch[-1]
            dep.note_w(wt, *ks, 'g_s', 'g_m', 'g_t1', 'g_r')
            dep.note_r(wt, 'prb')

        pend_ln = None
        for i in range(NT):
            par = i % 2
            xr = x_load(sq, i, res[:, i, :], f'res:{i}')
            pq = pq0 if par == 0 else pq1
            pks = (f'pq{par}lo', f'pq{par}hi')
            dep.rd('tensor', 'wbufB', 'biaspg', f'fmc:{i // 4}')
            dep.wr('tensor', pks[0], pks[1])
            for p in range(NPAIR):
                for oh in range(2):
                    E('tensor', lambda e, i=i, p=p, oh=oh, pq=pq:
                      e.matmul(pq[:, 512 * oh:512 * (oh + 1)],
                               fm[:, p, 128 * i:128 * (i + 1)],
                               wbufB[:, p, 512 * oh:512 * (oh + 1)],
                               start=(p == 0), stop=False))
            for oh in range(2):
                tk = E('tensor', lambda e, oh=oh, pq=pq:
                       e.matmul(pq[:, 512 * oh:512 * (oh + 1)], ones_bf[0:1, :],
                                biaspg[0:1, 3 * DM + 512 * oh:3 * DM + 512 * (oh + 1)],
                                start=False, stop=True), s_t if oh == 1 else None)
            dep.note_w(tk, pks[0], pks[1])
            dep.note_r(tk, 'wbufB', f'fmc:{i // 4}')
            wait('vector', tk)
            wait('vector', xr)
            dep.wr('vector', f'res:{i}')
            E('vector', lambda e, i=i, pq=pq: e.tensor_tensor(
                res[:, i, 0:512], pq[:, 0:512], res[:, i, 0:512], ALU.add))
            ra = E('vector', lambda e, i=i, pq=pq: e.tensor_tensor(
                res[:, i, 512:1024], pq[:, 512:1024], res[:, i, 512:1024], ALU.add), s_v)
            dep.note_w(ra, f'res:{i}')
            dep.note_r(ra, pks[0], pks[1])
            xm, mbuf, mk = ln_A(i, res[:, i, :], (f'res:{i}',), par)
            vv = ln_S(i, par, xm, mbuf, mk)
            if pend_ln is not None:
                pi = pend_ln[0]
                ln_B(*pend_ln)
                gtk = gate_mms(pi)
                if pi % 4 == 3:
                    gate_chain(pi // 4, gtk)
            pend_ln = (i, par, vv, mbuf, mk, f'fmc:{i // 4}')
        ln_B(*pend_ln)
        gtk = gate_mms(NT - 1)
        gate_chain(NT // 4 - 1, gtk)

        for b in range(2):
            dep.rd('tensor', 'tables')
            dep.wr('tensor', 'pt0', 'pt1')
            for t8 in range(8):
                i = 8 * b + t8
                dep.rd('tensor', f'w_tok:{i}')
                tk = E('tensor', lambda e, i=i, t8=t8: e.transpose(
                    pt[0:8, 128 * t8:128 * (t8 + 1)], w_tok[:, i, :], ident_bf[:, :]),
                    s_t if t8 == 7 else None)
            dep.note_w(tk, 'pt0', 'pt1')
            for t8 in range(8):
                dep.note_r(tk, f'w_tok:{8 * b + t8}')
            wait('vector', tk)
            dep.wr('vector', 'w_fm')
            wf = E('vector', lambda e, b=b: e.tensor_copy(
                w_fm[0:8, 1024 * b:1024 * (b + 1)], pt[0:8, 0:1024]), s_v)
            dep.note_r(wf, 'pt0', 'pt1')
            dep.note_w(wf, 'w_fm')

        if sq == 0:
            dump('dbg_w', w_tok[:, :, :], *[f'w_tok:{i}' for i in range(NT)])
            dump('dbg_fm2', fm[:, :, :], *[f'fmc:{c}' for c in range(4)])
        if not phase_gate('ffn'):
            continue

        def wrep_for(e_n):
            dep.rd('tensor', 'w_fm', 'tables')
            WRB = [(pv[:, 0:512], 'pv'), (pvo[:, 0:512], 'pvo'),
                   (pq1[:, 0:512], 'pq1lo'), (pq1[:, 512:1024], 'pq1hi')]
            for cc in range(4):
                dep.wr('tensor', WRB[cc][1])
            for cc in range(4):
                bc = E('tensor', lambda e, e_n=e_n, cc=cc, pz=WRB[cc][0]:
                       e.matmul(pz[:, :], sel[:, 128 * e_n:128 * (e_n + 1)],
                                w_fm[0:8, 512 * cc:512 * (cc + 1)], start=True, stop=True),
                       s_t if cc == 3 else None)
            for cc in range(4):
                dep.note_w(bc, WRB[cc][1])
            dep.note_r(bc, 'w_fm')
            wait('vector', bc)
            dep.wr('vector', 'wrep')
            for cc in range(4):
                wr_t = E('vector', lambda e, cc=cc, pz=WRB[cc][0]:
                         e.tensor_copy(wrep[:, 512 * cc:512 * (cc + 1)], pz[:, :]),
                         s_v if cc == 3 else None)
            dep.note_w(wr_t, 'wrep')
            for cc in range(4):
                dep.note_r(wr_t, WRB[cc][1])

        wrep_for(0)
        fill(w1bufA, 'w1bufA', w1_slices(0, 1), s_wa)
        fill(w1bufB, 'w1bufB', w1_slices(0, 0), s_wb)
        fill(w2bufA, 'w2bufA', w2_slices(0), s_w2a)
        for e_i in range(NEXP):
            routed = e_i < 8
            if e_i % 2 == 0:
                w1g_buf, w1g_key, s_g = w1bufA, 'w1bufA', s_wa
                w1y_buf, w1y_key, s_y = w1bufB, 'w1bufB', s_wb
                w2_buf, w2_key = w2bufA, 'w2bufA'
                w2n_buf, w2n_key, s_w2n = w2bufB, 'w2bufB', s_w2b
            else:
                w1g_buf, w1g_key, s_g = w1bufB, 'w1bufB', s_wb
                w1y_buf, w1y_key, s_y = w1bufA, 'w1bufA', s_wa
                w2_buf, w2_key = w2bufB, 'w2bufB'
                w2n_buf, w2n_key, s_w2n = w2bufA, 'w2bufA', s_w2a
            if debug and sq == 0 and e_i == 1:
                dump('dbg_w1A', w1bufA[:, :, :], 'w1bufA')
                dump('dbg_w1B', w1bufB[:, :, :], 'w1bufB')
                dump('dbg_w2', w2_buf[:, :, :], w2_key)
            W1BANKS = [(pq0, 0, 'pq0lo'), (pq0, 512, 'pq0hi'), (pq1, 0, 'pq1lo'), (pq1, 512, 'pq1hi')]
            for j in range(NJ):
                si_last = None
                for htc in range(2):
                    tcs = (2 * htc, 2 * htc + 1)
                    dep.rd('tensor', w1g_key, f'fmc:{tcs[0]}', f'fmc:{tcs[1]}')
                    for tc in tcs:
                        dep.wr('tensor', W1BANKS[tc][2])
                    for cch in range(DC):
                        for tc in tcs:
                            psc, sl, pk = W1BANKS[tc]
                            tk = E('tensor', lambda e, j=j, tc=tc, cch=cch, psc=psc, sl=sl, wb=w1g_buf:
                                   e.matmul(psc[:, sl:sl + 512], wb[:, cch, 128 * j:128 * (j + 1)],
                                            fm[:, cch, 512 * tc:512 * (tc + 1)],
                                            start=(cch == 0), stop=(cch == DC - 1)),
                                   s_t if (cch == DC - 1 and tc == tcs[1]) else None)
                    for tc in tcs:
                        psc, sl, pk = W1BANKS[tc]
                        dep.note_w(tk, pk)
                        dep.note_r(tk, w1g_key, f'fmc:{tc}')
                    wait('scalar', tk)
                    for tc in tcs:
                        psc, sl, pk = W1BANKS[tc]
                        dep.wr('scalar', f'act:{j}')
                        si = E('scalar', lambda e, j=j, tc=tc, psc=psc, sl=sl:
                               e.activation(act[:, j, 512 * tc:512 * (tc + 1)], psc[:, sl:sl + 512], AF.Silu), s_s)
                        dep.note_w(si, f'act:{j}')
                        dep.note_r(si, pk)
                        si_last = si
                if routed:
                    wait('vector', si_last)
                    dep.rd('vector', 'wrep')
                    dep.wr('vector', f'act:{j}')
                    aw = E('vector', lambda e, j=j: e.tensor_tensor(
                        act[:, j, :], act[:, j, :], wrep[:, :], ALU.mult), s_v)
                    dep.note_w(aw, f'act:{j}')
                    dep.note_r(aw, 'wrep')
            if e_i + 1 < NEXP:
                fill(w1g_buf, w1g_key, w1_slices(e_i + 1, 0), s_g)
            for j in range(NJ):
                for htc in range(2):
                    tcs = (2 * htc, 2 * htc + 1)
                    dep.rd('tensor', w1y_key, f'fmc:{tcs[0]}', f'fmc:{tcs[1]}')
                    for tc in tcs:
                        dep.wr('tensor', W1BANKS[tc][2])
                    for cch in range(DC):
                        for tc in tcs:
                            psc, sl, pk = W1BANKS[tc]
                            tk = E('tensor', lambda e, j=j, tc=tc, cch=cch, psc=psc, sl=sl, wb=w1y_buf:
                                   e.matmul(psc[:, sl:sl + 512], wb[:, cch, 128 * j:128 * (j + 1)],
                                            fm[:, cch, 512 * tc:512 * (tc + 1)],
                                            start=(cch == 0), stop=(cch == DC - 1)),
                                   s_t if (cch == DC - 1 and tc == tcs[1]) else None)
                    for tc in tcs:
                        psc, sl, pk = W1BANKS[tc]
                        dep.note_w(tk, pk)
                        dep.note_r(tk, w1y_key, f'fmc:{tc}')
                    wait('vector', tk)
                    for tc in tcs:
                        psc, sl, pk = W1BANKS[tc]
                        dep.wr('vector', f'act:{j}')
                        ya = E('vector', lambda e, j=j, tc=tc, psc=psc, sl=sl:
                               e.tensor_tensor(act[:, j, 512 * tc:512 * (tc + 1)], psc[:, sl:sl + 512],
                                               act[:, j, 512 * tc:512 * (tc + 1)], ALU.mult), s_v)
                        dep.note_w(ya, f'act:{j}')
                        dep.note_r(ya, pk)
            if debug and sq == 0 and e_i == 1:
                dump('dbg_act0', act[:, :, :], *[f'act:{j}' for j in range(NJ)])
            if e_i + 1 < NEXP:
                fill(w1y_buf, w1y_key, w1_slices(e_i + 1, 1), s_y)
                fill(w2n_buf, w2n_key, w2_slices(e_i + 1), s_w2a if w2n_key == 'w2bufA' else s_w2b)
                if e_i + 1 < 8:
                    wrep_for(e_i + 1)
            for i in range(NT):
                if i % 2 == 0:
                    zb = [(pv[:, 0:512], 'pv'), (pvo[:, 0:512], 'pvo')]
                else:
                    zb = [(pq1[:, 0:512], 'pq1lo'), (pq1[:, 512:1024], 'pq1hi')]
                dep.rd('tensor', w2_key, *[f'act:{j}' for j in range(NJ)])
                dep.wr('tensor', zb[0][1], zb[1][1])
                for j in range(NJ):
                    for oh in range(2):
                        tk = E('tensor', lambda e, i=i, oh=oh, j=j, pz=zb[oh][0], wb=w2_buf:
                               e.matmul(pz[:, :], act[:, j, 128 * i:128 * (i + 1)],
                                        wb[:, j, 512 * oh:512 * (oh + 1)],
                                        start=(j == 0), stop=(j == NJ - 1)),
                               s_t if (j == NJ - 1 and oh == 1) else None)
                dep.note_w(tk, zb[0][1], zb[1][1])
                dep.note_r(tk, w2_key, *[f'act:{j}' for j in range(NJ)])
                wait('vector', tk)
                dep.wr('vector', f'res:{i}')
                for oh in range(2):
                    za = E('vector', lambda e, i=i, oh=oh, pz=zb[oh][0]: e.tensor_tensor(
                        res[:, i, 512 * oh:512 * (oh + 1)], pz[:, :],
                        res[:, i, 512 * oh:512 * (oh + 1)], ALU.add), s_v if oh == 1 else None)
                za = (s_v, s_v.n)
                dep.note_w(za, f'res:{i}')
                dep.note_r(za, zb[0][1], zb[1][1])
                if e_i == NEXP - 1:
                    wait('gpsimd', za)
                    s_out.tok(16)
                    em('gpsimd', lambda e, sq=sq, i=i, h=s_out.h:
                       e.dma_start(dout[sq, 128 * i:128 * (i + 1), :], res[:, i, :]).then_inc(h, 16))
        out_tot = (s_out, s_out.n)
        for i in range(NT):
            dep.note_r(out_tot, f'res:{i}')
        dep.note_r(out_tot, 'q_fm')
        dep.note_r(out_tot, 'k_fm')
        if sq + 1 < nseq:
            fill(wbufA, 'wbufA', qkv_slices(swq), s_wa)

    if s_out.n == 0:
        dep.wr('vector', 'xs0')
        mz = E('vector', lambda e: e.memset(xslot[0][:, :], 0.0), s_v)
        dep.note_w(mz, 'xs0')
        wait('gpsimd', mz)
        s_out.tok(16)
        em('gpsimd', lambda e, h=s_out.h: e.dma_start(dout[0, 0:128, :], xslot[0][:, :]).then_inc(h, 16))
    em('gpsimd', lambda e, h=s_out.h, v=s_out.n: e.wait_ge(h, v))
    if debug and s_dbg.n:
        em('gpsimd', lambda e, h=s_dbg.h, v=s_dbg.n: e.wait_ge(h, v))

    with nc.Block() as block:
        for name in ENGS:
            lst = P.ops[name]
            getattr(block, name)(lambda eng, lst=lst: [f(eng) for f in lst])

    stack.close()
    nc.finalize()
    return nc


_CACHE = {}


def _get_nc(nseq=NSEQ):
    if nseq not in _CACHE:
        _CACHE[nseq] = build_program(nseq)
    return _CACHE[nseq]


def make_tables():
    dh = DH
    inv_freq = (1.0 / (np.float32(10000.0) ** (np.arange(0, dh, 2, dtype=np.float32) / np.float32(dh)))).astype(np.float32)
    t = np.arange(S, dtype=np.float32)
    freqs = (t[:, None] * inv_freq[None, :]).astype(np.float32)
    emb = np.concatenate([freqs, freqs], axis=-1)
    return dict(
        cos_t=np.cos(emb).astype(np.float32),
        sin_t=np.sin(emb).astype(np.float32),
        ident_bf=np.eye(128, dtype=ml_dtypes.bfloat16),
        causal_bf=np.triu(np.ones((128, 128), dtype=np.float32)).astype(ml_dtypes.bfloat16),
        sel_bf=_sel(),
    )


def _sel():
    sel = np.zeros((8, 1024), dtype=ml_dtypes.bfloat16)
    for e in range(8):
        sel[e, 128 * e:128 * (e + 1)] = 1.0
    return sel


def make_in_map(x_shard, W):
    m = dict(W)
    m['x'] = x_shard
    return m


def kernel(L0_attn_x, L0_attn_Wq, L0_attn_bq, L0_attn_Wk, L0_attn_bk,
           L0_attn_Wv, L0_attn_bv, L0_attn_Wo, L0_attn_bo,
           L0_ffn_gate_w, L0_ffn_W1, L0_ffn_W2, L0_ffn_Ws1, L0_ffn_Ws2):
    x = np.ascontiguousarray(np.asarray(L0_attn_x, dtype=np.float32))
    per = x.shape[0] // NCORES
    assert per == NSEQ
    nc = _get_nc(NSEQ)
    W = dict(
        wq=np.ascontiguousarray(L0_attn_Wq, dtype=np.float32),
        wk=np.ascontiguousarray(L0_attn_Wk, dtype=np.float32),
        wv=np.ascontiguousarray(L0_attn_Wv, dtype=np.float32),
        wo=np.ascontiguousarray(L0_attn_Wo, dtype=np.float32),
        bq=np.ascontiguousarray(L0_attn_bq, dtype=np.float32),
        bk=np.ascontiguousarray(L0_attn_bk, dtype=np.float32),
        bv=np.ascontiguousarray(L0_attn_bv, dtype=np.float32),
        bo=np.ascontiguousarray(L0_attn_bo, dtype=np.float32),
        gate_w=np.ascontiguousarray(L0_ffn_gate_w, dtype=np.float32),
        w1=np.ascontiguousarray(L0_ffn_W1, dtype=np.float32),
        w2=np.ascontiguousarray(L0_ffn_W2, dtype=np.float32),
        ws1=np.ascontiguousarray(L0_ffn_Ws1, dtype=np.float32),
        ws2=np.ascontiguousarray(L0_ffn_Ws2, dtype=np.float32),
        **make_tables(),
    )
    in_maps = [make_in_map(x[per * i:per * (i + 1)], W) for i in range(NCORES)]
    resu = run_bass_kernel_spmd(nc, in_maps, core_ids=list(range(NCORES)))
    return np.concatenate([np.asarray(r['out']) for r in resu.results], axis=0)


# version 16
# speedup vs baseline: 1.1106x; 1.1106x over previous
import contextlib
import numpy as np
import ml_dtypes
import concourse.bass as bass
import concourse.mybir as mybir
from concourse.bass_utils import run_bass_kernel_spmd

F32 = mybir.dt.float32
BF16 = mybir.dt.bfloat16
AF = mybir.ActivationFunctionType
ALU = mybir.AluOpType
AX = mybir.AxisListType
AP = bass.AP

NCORES = 8
NSEQ = 4
S = 2048
DM = 1024
NT = 16
NH = 16
NPAIR = 8
DH = 64
DC = 8
QC = 4
NEXP = 10
HE = 768
NJ = 6
EPS = 1e-5

ENGS = ('sync', 'gpsimd', 'scalar', 'vector', 'tensor')

ALIAS = {
    'wbufA': ['w1bufA', 'w1bufB', 'rbA1', 'rbB1'],
    'wbufB': ['w1bufB'],
    'w1bufA': ['wbufA', 'rbA1', 'rbB1'],
    'w1bufB': ['wbufA', 'wbufB'],
    'w2bufA': ['xs0', 'xs1', 'msc1', 'lnbf2'],
    'w2bufB': ['msc2', 'lnbf', 'expt0', 'expt1', 'expt2', 'expt3', 'rbA', 'rbB'],
    'xs0': ['w2bufA', 'lnbf2', 'lnbf2b'], 'xs1': ['w2bufA', 'qsbb', 'lnbf2c'], 'msc1': ['w2bufA', 'msc1b'],
    'lnbf2': ['w2bufA', 'xs0'], 'lnbf2b': ['w2bufA', 'xs0'], 'lnbf2c': ['w2bufA', 'xs1', 'qsbb'],
    'msc2': ['w2bufB', 'msc2b', 'lnbf2e'], 'lnbf2e': ['w2bufB', 'msc2'], 'msc2b': ['w2bufB', 'msc2'], 'msc1b': ['w2bufA', 'msc1'], 'qsbb': ['w2bufA', 'xs1'], 'lnbf': ['w2bufB', 'rbB'],
    'expt0': ['w2bufB', 'gnat'], 'expt1': ['w2bufB', 'gnat'], 'expt2': ['w2bufB'], 'expt3': ['w2bufB'],
    'rbA': ['w2bufB'], 'rbB': ['lnbf', 'w2bufB'],
    'rbA1': ['wbufA', 'w1bufA'], 'rbB1': ['wbufA', 'w1bufA'],
    'gnat': ['expt0', 'expt1', 'w2bufB'],
    'q_fm': [f'res:{i}' for i in range(8)],
    'k_fm': [f'res:{i}' for i in range(8, 16)],
    'v_aug': [f'act:{j}' for j in range(NJ)] + ['wrep', 'w_fm', 'dumpb'],
    'dumpb': ['v_aug', 'act:0'],
    'wrep': ['v_aug'], 'w_fm': ['v_aug'],
}
for i in range(8):
    ALIAS[f'res:{i}'] = ['q_fm']
for i in range(8, 16):
    ALIAS[f'res:{i}'] = ['k_fm']
for j in range(NJ):
    ALIAS[f'act:{j}'] = ['v_aug']
ALIAS['act:0'].append('dumpb')


class Sem:
    def __init__(self, handle, name):
        self.h = handle
        self.name = name
        self.n = 0

    def tok(self, k=1):
        self.n += k
        return (self, self.n)


class Prog:
    def __init__(self):
        self.ops = {e: [] for e in ENGS}
        self.waited = {e: {} for e in ENGS}
        self.own = {}

    def em(self, eng, fn):
        self.ops[eng].append(fn)

    def wait(self, eng, tok, force=False):
        if tok is None:
            return
        sem, val = tok
        assert val <= sem.n, f"forward wait {sem.name}:{val}>{sem.n}"

        w = self.waited[eng]
        if w.get(sem.name, -1) >= val:
            return
        w[sem.name] = val
        self.em(eng, lambda e, h=sem.h, v=val: e.wait_ge(h, v))


class Deps:

    def __init__(self, prog):
        self.p = prog
        self.d = {}

    def _expand(self, keys):
        out = list(keys)
        for k in keys:
            out.extend(ALIAS.get(k, ()))
        return out

    def rd(self, eng, *keys):
        for k in keys:
            e = self.d.get(k)
            if e:
                for t in e['w']:
                    self.p.wait(eng, t)

    def wr(self, eng, *keys):
        for k in self._expand(keys):
            e = self.d.get(k)
            if e:
                for t in e['w']:
                    self.p.wait(eng, t)
                for t in e['r']:
                    self.p.wait(eng, t)

    def note_w(self, tok, *keys):
        for k in keys:
            e = self.d.setdefault(k, {'w': [], 'r': []})
            e['w'].append(tok)
            if len(e['w']) > 24:
                e['w'] = e['w'][-24:]
            e['r'] = []

    def note_r(self, tok, *keys):
        for k in keys:
            e = self.d.setdefault(k, {'w': [], 'r': []})
            e['r'].append(tok)


def build_program(nseq=NSEQ, debug=False, stop_after=None, ln1_tiles=NT, ln_steps=99, pj_steps=99, skip_casts=False, skip_tab=False, skip_gate=False):
    nc = bass.Bass("TRN2", target_bir_lowering=False)

    dx = nc.dram_tensor('x', [nseq, S, DM], F32, kind='ExternalInput')
    dwq = nc.dram_tensor('wq', [DM, DM], F32, kind='ExternalInput')
    dwk = nc.dram_tensor('wk', [DM, DM], F32, kind='ExternalInput')
    dwv = nc.dram_tensor('wv', [DM, DM], F32, kind='ExternalInput')
    dwo = nc.dram_tensor('wo', [DM, DM], F32, kind='ExternalInput')
    dbq = nc.dram_tensor('bq', [DM], F32, kind='ExternalInput')
    dbk = nc.dram_tensor('bk', [DM], F32, kind='ExternalInput')
    dbv = nc.dram_tensor('bv', [DM], F32, kind='ExternalInput')
    dbo = nc.dram_tensor('bo', [DM], F32, kind='ExternalInput')
    dgate = nc.dram_tensor('gate_w', [8, DM], F32, kind='ExternalInput')
    dw1 = nc.dram_tensor('w1', [8, 2 * HE, DM], F32, kind='ExternalInput')
    dw2 = nc.dram_tensor('w2', [8, DM, HE], F32, kind='ExternalInput')
    dws1 = nc.dram_tensor('ws1', [4 * HE, DM], F32, kind='ExternalInput')
    dws2 = nc.dram_tensor('ws2', [DM, 2 * HE], F32, kind='ExternalInput')
    dcos = nc.dram_tensor('cos_t', [S, DH], F32, kind='ExternalInput')
    dsin = nc.dram_tensor('sin_t', [S, DH], F32, kind='ExternalInput')
    dident_bf = nc.dram_tensor('ident_bf', [128, 128], BF16, kind='ExternalInput')
    dcausal = nc.dram_tensor('causal_bf', [128, 128], BF16, kind='ExternalInput')
    dsel = nc.dram_tensor('sel_bf', [8, 1024], BF16, kind='ExternalInput')
    dout = nc.dram_tensor('out', [nseq, S, DM], F32, kind='ExternalOutput')
    if debug:
        ddbg = {k: nc.dram_tensor(k, shp, F32, kind='ExternalOutput') for k, shp in [
            ('dbg_fm1', [128, DC, S]), ('dbg_q', [128, NPAIR, S]), ('dbg_k', [128, NPAIR, S]),
            ('dbg_v', [128, NT, NH * 65]), ('dbg_c', [128, 1]), ('dbg_attn', [128, DC, S]),
            ('dbg_res1', [128, NT, DM]), ('dbg_w', [128, NT, 8]), ('dbg_fm2', [128, DC, S]),
            ('dbg_act0', [128, NJ, S]), ('dbg_w1A', [128, DC, HE]), ('dbg_w1B', [128, DC, HE]), ('dbg_w2', [128, NJ, DM])]}

    swq = nc.dram_tensor('swq', [DM, DM], BF16, kind='Internal')
    swk = nc.dram_tensor('swk', [DM, DM], BF16, kind='Internal')
    swv = nc.dram_tensor('swv', [DM, DM], BF16, kind='Internal')
    swo = nc.dram_tensor('swo', [DM, DM], BF16, kind='Internal')
    sw1 = nc.dram_tensor('sw1', [8, 2 * HE, DM], BF16, kind='Internal')
    sw2 = nc.dram_tensor('sw2', [8, DM, HE], BF16, kind='Internal')
    sws1 = nc.dram_tensor('sws1', [4 * HE, DM], BF16, kind='Internal')
    sws2 = nc.dram_tensor('sws2', [DM, 2 * HE], BF16, kind='Internal')
    sgate = nc.dram_tensor('sgate', [8, DM], BF16, kind='Internal')

    BASE = (nc.sbuf_base + 63) // 64 * 64

    def sb(name, shape, dtype, at):
        return nc.alloc_sbuf_tensor_at(name, shape, dtype, offset=BASE + at)

    res = sb('res', [128, NT, DM], F32, 0)
    q_fm = sb('q_fm', [128, NPAIR, S], BF16, 0)
    k_fm = sb('k_fm', [128, NPAIR, S], BF16, 32768)
    fm = sb('fm', [128, DC, S], BF16, 65536)
    VB = 98304
    v_aug = sb('v_aug', [128, NT, NH * 65], BF16, VB)
    dumpb = sb('dumpb', [128, DM], BF16, VB)
    act = sb('act', [128, NJ, S], BF16, VB)
    wrep = sb('wrep', [128, S], BF16, VB + 24576)
    w_fm = sb('w_fm', [8, S], BF16, VB + 28672)
    WB = 131584
    wbufA = sb('wbufA', [128, DC, DM], BF16, WB)
    wbufB = sb('wbufB', [128, DC, DM], BF16, WB + 16384)
    rbA1 = sb('rbA1', [128, 512], F32, WB)
    rbB1 = sb('rbB1', [128, 512], F32, WB + 2048)
    w1bufA = sb('w1bufA', [128, DC, HE], BF16, WB)
    w1bufB = sb('w1bufB', [128, DC, HE], BF16, WB + 12288)
    PB = 164352
    xslot = [sb('xslot0', [128, DM], F32, PB),
             sb('xslot1', [128, DM], F32, PB + 4096)]
    msc1 = sb('msc1', [128, DM], F32, PB + 8192)
    msc2 = sb('msc2', [128, DM], F32, PB + 12288)
    msc1b = sb('msc1b', [128, DM], BF16, PB + 8192)
    msc2b = sb('msc2b', [128, DM], BF16, PB + 12288)
    qsbb = sb('qsbb', [128, DM], BF16, PB + 4096)
    lnbf = sb('lnbf', [128, DM], BF16, PB + 16384)
    lnbf2 = sb('lnbf2', [128, DM], BF16, PB)
    lnbf2b = sb('lnbf2b', [128, DM], BF16, PB + 2048)
    lnbf2c = sb('lnbf2c', [128, DM], BF16, PB + 6144)
    lnbf2d = sb('lnbf2d', [128, DM], BF16, PB + 8192 + 2048)
    lnbf2e = sb('lnbf2e', [128, DM], BF16, PB + 12288 + 2048)
    expt = [sb(f'expt{i}', [128, 512], BF16, PB + 18432 + 1024 * i) for i in range(4)]
    gnat = sb('gnat', [8, 1024], BF16, PB + 18432)
    rbB = sb('rbB', [128, 512], F32, PB + 16384)
    rbA = sb('rbA', [128, 512], F32, PB + 22528)
    w2bufA = sb('w2bufA', [128, NJ, DM], BF16, PB)
    w2bufB = sb('w2bufB', [128, NJ, DM], BF16, PB + 12288)
    off = PB + 24576
    biaspg = sb('biaspg', [1, 4 * DM], BF16, off); off += 8192
    cosr = sb('cosr', [128, NT, DH], BF16, off); off += 2048
    sinr = sb('sinr', [128, NT, DH], BF16, off); off += 2048
    ident_bf = sb('ident_bf', [128, 128], BF16, off); off += 256
    causal = sb('causal', [128, 128], BF16, off); off += 256
    sel = sb('sel', [8, 1024], BF16, off); off += 2048
    gateT = sb('gateT', [128, DC, 16], BF16, off); off += 512
    ones_bf = sb('ones_bf', [1, 128], BF16, off); off += 256
    ones_col = sb('ones_col', [128, 1], BF16, off); off += 32
    ones32 = sb('ones32', [128, 64], F32, off); off += 256
    stat_nm = sb('stat_nm', [128, 2], F32, off); off += 32
    stat_rs = sb('stat_rs', [128, 2], F32, off); off += 32
    stat_v1 = sb('stat_v1', [128, 4], F32, off); off += 32
    qn2 = sb('qn2', [128, NT], F32, off); off += 64
    qn2m = sb('qn2m', [128, 1], BF16, off); off += 32
    kn2m = sb('kn2m', [128, 1], BF16, off); off += 32
    cnst = sb('cnst', [1, 16], F32, off); off += 64
    negC8 = sb('negC8', [128, 1], F32, off); off += 32
    g_s = sb('g_s', [128, 32], F32, off); off += 128
    g_t1 = sb('g_t1', [128, 32], F32, off); off += 128
    g_m = sb('g_m', [128, 4], F32, off); off += 32
    g_r = sb('g_r', [128, 4], F32, off); off += 32
    w_tok = sb('w_tok', [128, NT, 8], BF16, off); off += 256
    assert BASE + off <= 229000, (BASE, off)
    nc.sbuf_base = BASE + off

    pq0 = nc.alloc_psum_tensor('pq0', [128, 1024], F32)
    pq1 = nc.alloc_psum_tensor('pq1', [128, 1024], F32)
    pv = nc.alloc_psum_tensor('pv', [128, 512], F32)
    pvo = nc.alloc_psum_tensor('pvo', [128, 512], F32)
    pt = nc.alloc_psum_tensor('pt', [128, 1024], BF16)
    prb = nc.alloc_psum_tensor('prb', [128, 512], F32)

    P = Prog()
    dep = Deps(P)
    stack = contextlib.ExitStack()

    def sem(name):
        return Sem(stack.enter_context(nc.semaphore(name)), name)

    s_scr = sem('s_scr')
    s_x = [sem('s_x0'), sem('s_x1')]
    s_out = sem('s_out')
    s_tab = sem('s_tab')
    s_wa = sem('s_wa')
    s_wb = sem('s_wb')
    s_w2a = sem('s_w2a')
    s_w2b = sem('s_w2b')
    s_dbg = sem('s_dbg')
    s_t = sem('s_t')
    s_v = sem('s_v')
    s_s = sem('s_s')
    s_gp = sem('s_gp')
    P.own = {'tensor': s_t, 'vector': s_v, 'scalar': s_s, 'gpsimd': s_gp}

    em, wait = P.em, P.wait

    def E(eng, fn, inc=None):
        if inc is None:
            em(eng, fn)
            return None
        t = inc.tok()
        em(eng, lambda e, fn=fn, h=inc.h: fn(e).then_inc(h, 1))
        return t

    def gp_dma(dst, src, semx):
        t = semx.tok(16)
        em('gpsimd', lambda e, d=dst, s=src, h=semx.h: e.dma_start(d, s).then_inc(h, 16))
        return t

    if not skip_casts:
        gp_dma(swq[:, :], dwq[:, :], s_scr)
        gp_dma(swk[:, :], dwk[:, :], s_scr)
        gp_dma(swv[:, :], dwv[:, :], s_scr)
        gp_dma(swo[:, :], dwo[:, :], s_scr)
        gp_dma(sw1[:, :, :], dw1[:, :, :], s_scr)
        gp_dma(sw2[:, :, :], dw2[:, :, :], s_scr)
        gp_dma(sws1[:, :], dws1[:, :], s_scr)
        gp_dma(sws2[:, :], dws2[:, :], s_scr)
    gp_dma(sgate[:, :], dgate[:, :], s_scr)
    gp_dma(biaspg[0:1, 0:DM], dbq.ap().rearrange('(a b) -> a b', a=1), s_scr)
    gp_dma(biaspg[0:1, DM:2 * DM], dbk.ap().rearrange('(a b) -> a b', a=1), s_scr)
    gp_dma(biaspg[0:1, 2 * DM:3 * DM], dbv.ap().rearrange('(a b) -> a b', a=1), s_scr)
    gp_dma(biaspg[0:1, 3 * DM:4 * DM], dbo.ap().rearrange('(a b) -> a b', a=1), s_scr)
    gp_dma(gnat[:, :], dgate[:, :], s_scr)
    gp_dma(cosr[:, :, :], AP(dcos, 0, [[DH, 128], [128 * DH, NT], [1, DH]]), s_scr)
    gp_dma(sinr[:, :, :], AP(dsin, 0, [[DH, 128], [128 * DH, NT], [1, DH]]), s_scr)
    scr_done = (s_scr, s_scr.n)
    dep.note_w(scr_done, 'biaspg', 'gnat')

    def sy_dma(dst, src, semx, transpose=False):
        t = semx.tok(16)
        em('sync', lambda e, d=dst, s=src, h=semx.h, tr=transpose:
           e.dma_start(d, s, transpose=tr).then_inc(h, 16))
        return t

    if not skip_tab:
        sy_dma(causal[:, :], dcausal[:, :], s_tab)
        sy_dma(sel[:, :], dsel[:, :], s_tab)
    sy_dma(ident_bf[:, :], dident_bf[:, :], s_tab)
    tab_done = (s_tab, s_tab.n)
    dep.note_w(tab_done, 'tables')
    dep.note_w(scr_done, 'tables')
    if not skip_gate:
        wait('tensor', scr_done)
        wait('tensor', tab_done)
        dep.wr('tensor', 'pt0', 'pt1')
        for c in range(DC):
            gt = E('tensor', lambda e, c=c: e.transpose(pt[:, 8 * c:8 * (c + 1)], gnat[0:8, 128 * c:128 * (c + 1)], ident_bf[0:8, 0:8]),
                   s_t if c == DC - 1 else None)
        dep.note_w(gt, 'pt0')
        dep.note_r(gt, 'gnat')
        wait('vector', gt)
        gte = E('vector', lambda e: e.tensor_copy(
            AP(gateT, 0, [[DC * 16, 128], [16, DC], [1, 8]]),
            pt[:, 0:64].rearrange('p (a b) -> p a b', b=8)), s_v)
        dep.note_w(gte, 'gateT')
        dep.note_r(gte, 'pt0')

    E('vector', lambda e: e.memset(ones_bf[:, :], 1.0))
    E('vector', lambda e: e.memset(ones_col[:, :], 1.0))
    t_ones = E('vector', lambda e: e.memset(ones32[:, :], 1.0), s_v)
    dep.note_w(t_ones, 'ones')

    def fill(dst_sb, key, src_slices, semx):
        dep.wr('sync', key)
        wait('sync', scr_done)
        for c, sap in enumerate(src_slices):
            n = sap.shape[0]
            sy_dma(dst_sb[:, c, 0:n], sap, semx, transpose=True)
        tok = (semx, semx.n)
        dep.note_w(tok, key)
        return tok

    def qkv_slices(scr):
        return [scr[0:DM, 128 * c:128 * (c + 1)] for c in range(DC)]

    def w1_slices(eidx, half):
        out = []
        for c in range(DC):
            if eidx < 8:
                out.append(sw1[eidx, half * HE:(half + 1) * HE, 128 * c:128 * (c + 1)])
            else:
                r0 = half * 2 * HE + (eidx - 8) * HE
                out.append(sws1[r0:r0 + HE, 128 * c:128 * (c + 1)])
        return out

    def w2_slices(eidx):
        out = []
        for j in range(NJ):
            if eidx < 8:
                out.append(sw2[eidx, 0:DM, 128 * j:128 * (j + 1)])
            else:
                c0 = (eidx - 8) * HE + 128 * j
                out.append(sws2[0:DM, c0:c0 + 128])
        return out

    def x_load(sq, i, dst, dkey):
        par = i % 2
        sx = s_x[par]
        dep.wr('gpsimd', dkey)
        wait('gpsimd', (sx, sx.n) if sx.n else None)
        t = sx.tok(16)
        em('gpsimd', lambda e, sq=sq, i=i, d=dst, h=sx.h:
           e.dma_start(d, dx[sq, 128 * i:128 * (i + 1), :]).then_inc(h, 16))
        dep.note_w(t, dkey)
        return t

    def transpose_to(dst, dsti_off, src_tile, key_out, no_evac=False, src_key='lnbf'):
        dep.rd('tensor', 'tables')
        dep.wr('tensor', 'pt0', 'pt1')
        for cc in range(8):
            fn = lambda e, cc=cc, st=src_tile: e.transpose(
                pt[:, 128 * cc:128 * (cc + 1)],
                st[:, 128 * cc:128 * (cc + 1)], ident_bf[:, :])
            tk = E('tensor', fn, s_t if cc == 7 else None)
        dep.note_w(tk, 'pt0', 'pt1')
        dep.note_r(tk, src_key)
        if no_evac:
            return tk
        wait('vector', tk)
        dep.wr('vector', key_out)
        ev = E('vector', lambda e, o=dsti_off, d=dst: e.tensor_copy(
            AP(d, o, [[DC * S, 128], [S, DC], [1, 128]]),
            pt[:, :].rearrange('p (a b) -> p a b', b=128)), s_v)
        dep.note_r(ev, 'pt0', 'pt1')
        dep.note_w(ev, key_out)
        return ev

    def ln_A(i, src, sk, par):
        stn = stat_nm[:, par:par + 1]
        mbuf = msc1 if par == 0 else msc2
        mk = 'msc1' if par == 0 else 'msc2'
        dep.rd('vector', *sk)
        dep.wr('vector', f'stat_nm{par}')
        t = E('vector', lambda e, s=src, stn=stn: e.reduce_sum(stn, s, axis=AX.X, negate=True), s_v)
        P.wait('vector', t, force=True)
        nm = E('vector', lambda e, stn=stn: e.tensor_scalar_mul(stn, stn, 1.0 / DM), s_v)
        dep.note_w(nm, f'stat_nm{par}')
        P.wait('vector', nm, force=True)
        dep.wr('vector', mk)
        xm = E('vector', lambda e, s=src, stn=stn, mbuf=mbuf: e.tensor_scalar_add(mbuf[:, :], s, stn), s_v)
        dep.note_w(xm, mk)
        dep.note_r(xm, *sk, f'stat_nm{par}')
        return xm, mbuf, mk

    def ln_S(i, par, xm, mbuf, mk):
        wait('scalar', xm)
        dep.wr('scalar', 'dumpb', f'stat_v1{par}')
        vv = E('scalar', lambda e, mbuf=mbuf, par=par: e.activation(
            dumpb[:, :], mbuf[:, :], AF.Square, accum_out=stat_v1[:, par:par + 1]), s_s)
        dep.note_w(vv, 'dumpb', f'stat_v1{par}')
        dep.note_r(vv, mk)
        return vv

    def ln_B(i, par, vv, mbuf, mk, key_out):
        srs = stat_rs[:, par:par + 1]
        sv1 = stat_v1[:, par:par + 1]
        lnb = lnbf if par == 0 else lnbf2
        lbk = 'lnbf' if par == 0 else 'lnbf2'
        wait('vector', vv)
        dep.wr('vector', f'stat_rs{par}', f'stat_v1{par}')
        t = E('vector', lambda e, sv1=sv1: e.tensor_scalar(sv1, sv1, 1.0 / DM, EPS, ALU.mult, ALU.add), s_v)
        P.wait('vector', t, force=True)
        r1 = E('vector', lambda e, srs=srs, sv1=sv1: e.reciprocal(srs, sv1), s_v)
        dep.note_w(r1, f'stat_v1{par}')
        wait('scalar', r1)
        rs = E('scalar', lambda e, srs=srs: e.activation(srs, srs, AF.Sqrt), s_s)
        dep.note_w(rs, f'stat_rs{par}')
        wait('vector', rs)
        dep.wr('vector', lbk)
        ln = E('vector', lambda e, mbuf=mbuf, srs=srs, lnb=lnb: e.tensor_scalar_mul(lnb[:, :], mbuf[:, :], srs), s_v)
        dep.note_w(ln, lbk)
        dep.note_r(ln, mk, f'stat_rs{par}')
        wait('tensor', ln)
        transpose_to(fm, 128 * i, lnb, key_out, src_key=lbk)

    def ln_pass(sq, src_fn, src_key_fn, with_load, ntiles=NT):
        pend = None
        for i in range(ntiles):
            if with_load:
                if i == 0:
                    x_load(sq, 0, xslot[0][:, :], 'xs0')
                    x_load(sq, 1, xslot[1][:, :], 'xs1')
                elif i + 1 < ntiles:
                    x_load(sq, i + 1, xslot[(i + 1) % 2][:, :], f'xs{(i + 1) % 2}')
            par = i % 2
            xm, mbuf, mk = ln_A(i, src_fn(i), src_key_fn(i), par)
            vv = ln_S(i, par, xm, mbuf, mk)
            if pend is not None:
                ln_B(*pend)
            pend = (i, par, vv, mbuf, mk, f'fmc:{i // 4}')
        ln_B(*pend)

    def rope_chain(i, pq, pks, kind, qbuf, qbk):
        cos_ap = AP(cosr, i * DH, [[NT * DH, 128], [0, NH], [1, DH]])
        sin_ap = AP(sinr, i * DH, [[NT * DH, 128], [0, NH], [1, DH]])
        dep.rd('scalar', *pks)
        dep.wr('scalar', 'qsbb')
        qv = E('scalar', lambda e, pq=pq: e.copy(qsbb[:, :], pq[:, :]), s_s)
        dep.note_w(qv, 'qsbb')
        dep.note_r(qv, *pks)
        wait('vector', qv)
        dep.rd('vector', 'tables')
        dep.wr('vector', 'msc1b')
        t1 = E('vector', lambda e, ca=cos_ap: e.tensor_tensor(
            msc1b[:, :].rearrange('p (h d) -> p h d', d=DH),
            qsbb[:, :].rearrange('p (h d) -> p h d', d=DH), ca, ALU.mult), s_v)
        dep.note_w(t1, 'msc1b')
        dep.wr('scalar', 'msc2b')
        P.wait('scalar', qv, force=True)
        E('scalar', lambda e: e.activation(
            AP(msc2b, 0, [[DM, 128], [DH, NH], [1, 32]]),
            AP(qsbb, 32, [[DM, 128], [DH, NH], [1, 32]]), AF.Copy, scale=-1.0))
        rot = E('scalar', lambda e: e.activation(
            AP(msc2b, 32, [[DM, 128], [DH, NH], [1, 32]]),
            AP(qsbb, 0, [[DM, 128], [DH, NH], [1, 32]]), AF.Copy), s_s)
        dep.note_w(rot, 'msc2b')
        dep.note_r(rot, 'qsbb')
        dep.note_r(t1, 'qsbb')
        wait('vector', rot)
        sm = E('vector', lambda e, sa=sin_ap: e.tensor_tensor(
            msc2b[:, :].rearrange('p (h d) -> p h d', d=DH),
            msc2b[:, :].rearrange('p (h d) -> p h d', d=DH), sa, ALU.mult), s_v)
        dep.wr('vector', qbk)
        P.wait('vector', t1, force=True)
        P.wait('vector', sm, force=True)
        qr = E('vector', lambda e, qbuf=qbuf: e.tensor_tensor(qbuf[:, :], msc1b[:, :], msc2b[:, :], ALU.add), s_v)
        dep.note_w(qr, qbk)
        dep.note_r(qr, 'msc1b', 'msc2b')
        acc = qn2m if kind == 'q' else kn2m
        acck = 'qn2m' if kind == 'q' else 'kn2m'
        dep.wr('vector', 'msc1b', 'qn2', acck)
        P.wait('vector', qr, force=True)
        t = E('vector', lambda e, qbuf=qbuf: e.tensor_tensor(msc1b[:, :], qbuf[:, :], qbuf[:, :], ALU.mult), s_v)
        P.wait('vector', t, force=True)
        t = E('vector', lambda e: e.reduce_sum(
            qn2[:, :], msc1b[:, :].rearrange('p (h d) -> p h d', d=DH), axis=AX.X), s_v)
        if i == 0:
            E('vector', lambda e, acc=acc: e.memset(acc[:, :], 0.0))
        P.wait('vector', t, force=True)
        t = E('vector', lambda e: e.reduce_max(stat_v1[:, 0:1], qn2[:, :], axis=AX.X), s_v)
        P.wait('vector', t, force=True)
        n2 = E('vector', lambda e, acc=acc: e.tensor_tensor(acc[:, :], acc[:, :], stat_v1[:, 0:1], ALU.max), s_v)
        dep.note_w(n2, acck, 'msc1b', 'stat_v1')
        return qr

    def sweep_qk(sq):
        pend = [None, None]
        QB = [(lnbf, 'lnbf'), (lnbf2, 'lnbf2'), (lnbf2d, 'lnbf2d')]
        KB = [(lnbf2b, 'lnbf2b'), (lnbf2c, 'lnbf2c'), (lnbf2e, 'lnbf2e')]
        for i in range(NT):
            par = i % 3
            dep.rd('tensor', f'fmc:{i // 4}', 'wbufA', 'wbufB', 'biaspg')
            dep.wr('tensor', 'pq0lo', 'pq0hi', 'pq1lo', 'pq1hi')
            for c in range(DC):
                for wb, pq in ((wbufA, pq0), (wbufB, pq1)):
                    for oh in range(2):
                        E('tensor', lambda e, i=i, c=c, oh=oh, pq=pq, wb=wb:
                          e.matmul(pq[:, 512 * oh:512 * (oh + 1)],
                                   fm[:, c, 128 * i:128 * (i + 1)],
                                   wb[:, c, 512 * oh:512 * (oh + 1)],
                                   start=(c == 0), stop=False))
            for boff, pq in ((0, pq0), (DM, pq1)):
                for oh in range(2):
                    tk = E('tensor', lambda e, oh=oh, pq=pq, boff=boff:
                           e.matmul(pq[:, 512 * oh:512 * (oh + 1)], ones_bf[0:1, :],
                                    biaspg[0:1, boff + 512 * oh:boff + 512 * (oh + 1)],
                                    start=False, stop=True),
                           s_t if (oh == 1 and boff == DM) else None)
            dep.note_w(tk, 'pq0lo', 'pq0hi', 'pq1lo', 'pq1hi')
            dep.note_r(tk, f'fmc:{i // 4}', 'wbufA', 'wbufB')
            qrq = rope_chain(i, pq0, ('pq0lo', 'pq0hi'), 'q', *QB[par])
            qrk = rope_chain(i, pq1, ('pq1lo', 'pq1hi'), 'k', *KB[par])

            def do_tp(i=i, qrq=qrq, qrk=qrk, par=par):
                wait('tensor', qrq)
                if i == 0:
                    dep.wr('tensor', 'q_fm', 'k_fm')
                transpose_to(q_fm, 128 * i, QB[par][0], 'q_fm', src_key=QB[par][1])
                wait('tensor', qrk)
                transpose_to(k_fm, 128 * i, KB[par][0], 'k_fm', src_key=KB[par][1])
            if pend[0] is not None:
                pend[0]()
            pend[0] = pend[1]
            pend[1] = do_tp
        for pp in pend:
            if pp is not None:
                pp()

    def proj_sweep(sq, kind, wbuf, wkey):
        boff = {'q': 0, 'k': DM, 'v': 2 * DM}[kind]
        pend_tp = [None]
        for i in range(NT):
            par = i % 2
            pq = pq0 if par == 0 else pq1
            pks = (f'pq{par}lo', f'pq{par}hi')
            dep.rd('tensor', f'fmc:{i // 4}', wkey, 'biaspg')
            dep.wr('tensor', pks[0], pks[1])
            for c in range(DC):
                for oh in range(2):
                    E('tensor', lambda e, i=i, c=c, oh=oh, pq=pq, wb=wbuf:
                      e.matmul(pq[:, 512 * oh:512 * (oh + 1)],
                               fm[:, c, 128 * i:128 * (i + 1)],
                               wb[:, c, 512 * oh:512 * (oh + 1)],
                               start=(c == 0), stop=False))
            for oh in range(2):
                tk = E('tensor', lambda e, oh=oh, pq=pq, boff=boff:
                       e.matmul(pq[:, 512 * oh:512 * (oh + 1)], ones_bf[0:1, :],
                                biaspg[0:1, boff + 512 * oh:boff + 512 * (oh + 1)],
                                start=False, stop=True), s_t if oh == 1 else None)
            mm = tk
            dep.note_w(mm, pks[0], pks[1])
            dep.note_r(mm, f'fmc:{i // 4}', wkey)
            if pj_steps < 2:
                continue
            if kind in ('q', 'k'):
                cos_ap = AP(cosr, i * DH, [[NT * DH, 128], [0, NH], [1, DH]])
                sin_ap = AP(sinr, i * DH, [[NT * DH, 128], [0, NH], [1, DH]])
                wait('scalar', mm)
                dep.wr('scalar', 'qsbb')
                qv = E('scalar', lambda e, pq=pq: e.copy(qsbb[:, :], pq[:, :]), s_s)
                dep.note_w(qv, 'qsbb')
                dep.note_r(qv, *pks)
                wait('vector', qv)
                dep.rd('vector', 'tables')
                dep.wr('vector', 'msc1b')
                t1 = E('vector', lambda e, ca=cos_ap: e.tensor_tensor(
                    msc1b[:, :].rearrange('p (h d) -> p h d', d=DH),
                    qsbb[:, :].rearrange('p (h d) -> p h d', d=DH), ca, ALU.mult), s_v)
                dep.note_w(t1, 'msc1b')
                dep.wr('scalar', 'msc2b')
                P.wait('scalar', qv, force=True)
                E('scalar', lambda e: e.activation(
                    AP(msc2b, 0, [[DM, 128], [DH, NH], [1, 32]]),
                    AP(qsbb, 32, [[DM, 128], [DH, NH], [1, 32]]), AF.Copy, scale=-1.0))
                rot = E('scalar', lambda e: e.activation(
                    AP(msc2b, 32, [[DM, 128], [DH, NH], [1, 32]]),
                    AP(qsbb, 0, [[DM, 128], [DH, NH], [1, 32]]), AF.Copy), s_s)
                dep.note_w(rot, 'msc2b')
                dep.note_r(rot, 'qsbb')
                dep.note_r(t1, 'qsbb')
                wait('vector', rot)
                sm = E('vector', lambda e, sa=sin_ap: e.tensor_tensor(
                    msc2b[:, :].rearrange('p (h d) -> p h d', d=DH),
                    msc2b[:, :].rearrange('p (h d) -> p h d', d=DH), sa, ALU.mult), s_v)
                qbuf, qbk = (lnbf, 'lnbf') if i % 2 == 0 else (lnbf2, 'lnbf2')
                dep.wr('vector', qbk)
                P.wait('vector', t1, force=True)
                P.wait('vector', sm, force=True)
                qr = E('vector', lambda e, qbuf=qbuf: e.tensor_tensor(qbuf[:, :], msc1b[:, :], msc2b[:, :], ALU.add), s_v)
                dep.note_w(qr, qbk)
                dep.note_r(qr, 'msc1b', 'msc2b')
                if pj_steps < 3:
                    continue
                acc = qn2m if kind == 'q' else kn2m
                acck = 'qn2m' if kind == 'q' else 'kn2m'
                dep.wr('vector', 'msc1b', 'qn2', acck)
                P.wait('vector', qr, force=True)
                t = E('vector', lambda e, qbuf=qbuf: e.tensor_tensor(msc1b[:, :], qbuf[:, :], qbuf[:, :], ALU.mult), s_v)
                P.wait('vector', t, force=True)
                t = E('vector', lambda e: e.reduce_sum(
                    qn2[:, :], msc1b[:, :].rearrange('p (h d) -> p h d', d=DH), axis=AX.X), s_v)
                if i == 0:
                    E('vector', lambda e, acc=acc: e.memset(acc[:, :], 0.0))
                P.wait('vector', t, force=True)
                t = E('vector', lambda e: e.reduce_max(stat_v1[:, 0:1], qn2[:, :], axis=AX.X), s_v)
                P.wait('vector', t, force=True)
                n2 = E('vector', lambda e, acc=acc: e.tensor_tensor(acc[:, :], acc[:, :], stat_v1[:, 0:1], ALU.max), s_v)
                dep.note_w(n2, acck, 'msc1b', 'stat_v1')
                if pj_steps < 4:
                    continue
                dst = q_fm if kind == 'q' else k_fm
                dkey = 'q_fm' if kind == 'q' else 'k_fm'

                def do_tp(i=i, qr=qr, qbuf=qbuf, qbk=qbk, dst=dst, dkey=dkey):
                    wait('tensor', qr)
                    if i == 0:
                        dep.wr('tensor', dkey)
                    transpose_to(dst, 128 * i, qbuf, dkey, src_key=qbk)
                if pend_tp[0] is not None:
                    pend_tp[0]()
                pend_tp[0] = do_tp
            else:
                if pend_tp[0] is not None:
                    pend_tp[0]()
                    pend_tp[0] = None
                wait('scalar', mm)
                if i == 0:
                    dep.wr('scalar', 'v_aug')
                va = E('scalar', lambda e, i=i, pq=pq: e.activation(
                    AP(v_aug, i * (NH * 65), [[NT * NH * 65, 128], [65, NH], [1, 64]]),
                    pq[:, :].rearrange('p (h d) -> p h d', d=DH), AF.Copy), s_s)
                dep.note_w(va, 'v_aug')
                dep.note_r(va, *pks)
                wait('vector', va)
                vo = E('vector', lambda e, i=i: e.memset(
                    AP(v_aug, i * (NH * 65) + 64, [[NT * NH * 65, 128], [65, NH], [1, 1]]), 1.0), s_v)
                dep.note_w(vo, 'v_aug')
        if pend_tp[0] is not None:
            pend_tp[0]()
            pend_tp[0] = None

    def dump(name, buf, *keys):
        if not debug:
            return
        for k in keys:
            dep.rd('gpsimd', k)
        if s_dbg.n:
            wait('gpsimd', (s_dbg, s_dbg.n))
        t = s_dbg.tok(16)
        em('gpsimd', lambda e, d=ddbg[name], b=buf, h=s_dbg.h:
           e.dma_start(d.ap(), b).then_inc(h, 16))
        for k in keys:
            dep.note_r(t, k)

    _stopped = [False]

    def phase_gate(name):
        if _stopped[0]:
            return False
        if stop_after is not None and name == stop_after:
            _stopped[0] = True
        return True

    fill(wbufA, 'wbufA', qkv_slices(swq), s_wa)
    for sq in range(nseq):
        if not phase_gate('prelude'):
            break
        fill(wbufB, 'wbufB', qkv_slices(swk), s_wb)
        if phase_gate('ln1'):
            ln_pass(sq, lambda i: xslot[i % 2][:, :], lambda i: (f'xs{i % 2}',), True, ntiles=ln1_tiles)
        if sq == 0:
            dump('dbg_fm1', fm[:, :, :], *[f'fmc:{c}' for c in range(4)])
        if phase_gate('q') and phase_gate('k'):
            sweep_qk(sq)
            fill(wbufA, 'wbufA', qkv_slices(swv), s_wa)
        if phase_gate('v'):
            proj_sweep(sq, 'v', wbufA, 'wbufA')
            fill(wbufB, 'wbufB', qkv_slices(swo), s_wb)

        if sq == 0:
            dump('dbg_q', q_fm[:, :, :], 'q_fm')
            dump('dbg_k', k_fm[:, :, :], 'k_fm')
            dump('dbg_v', v_aug[:, :, :], 'v_aug')
        if not phase_gate('negc'):
            continue
        dep.rd('tensor', 'qn2m', 'kn2m', 'tables')
        dep.wr('tensor', 'pt0')
        E('tensor', lambda e: e.transpose(pt[0:1, 0:128], qn2m[:, :], ident_bf[:, :]))
        tC = E('tensor', lambda e: e.transpose(pt[0:1, 128:256], kn2m[:, :], ident_bf[:, :]), s_t)
        dep.note_w(tC, 'pt0')
        dep.note_r(tC, 'qn2m', 'kn2m')
        wait('vector', tC)
        dep.wr('vector', 'cnst')
        E('vector', lambda e: e.reduce_max(cnst[0:1, 0:1], pt[0:1, 0:128], axis=AX.X))
        t = E('vector', lambda e: e.reduce_max(cnst[0:1, 1:2], pt[0:1, 128:256], axis=AX.X), s_v)
        P.wait('vector', t, force=True)
        c3 = E('vector', lambda e: e.tensor_tensor(cnst[0:1, 2:3], cnst[0:1, 0:1], cnst[0:1, 1:2], ALU.mult), s_v)
        dep.note_r(c3, 'pt0')
        wait('scalar', c3)
        c4 = E('scalar', lambda e: e.activation(cnst[0:1, 3:4], cnst[0:1, 2:3], AF.Sqrt, scale=1.0 / 64.0), s_s)
        wait('vector', c4)
        c5 = E('vector', lambda e: e.tensor_scalar_mul(cnst[0:1, 4:5], cnst[0:1, 3:4], -1.0), s_v)
        dep.note_w(c5, 'cnst')
        wait('tensor', c5)
        dep.rd('tensor', 'ones')
        dep.wr('tensor', 'prb')
        E('tensor', lambda e: e.matmul(prb[0:64, 0:1], ones32[0:1, :], cnst[0:1, 4:5], start=True, stop=True))
        c6 = E('tensor', lambda e: e.matmul(prb[64:128, 1:2], ones32[0:1, :], cnst[0:1, 4:5], start=True, stop=True), s_t)
        dep.note_w(c6, 'prb')
        dep.note_r(c6, 'cnst', 'ones')
        wait('scalar', c6)
        dep.wr('scalar', 'negC8')
        E('scalar', lambda e: e.copy(negC8[0:64, :], prb[0:64, 0:1]))
        c7 = E('scalar', lambda e: e.copy(negC8[64:128, :], prb[64:128, 1:2]), s_s)
        dep.note_w(c7, 'negC8')
        dep.note_r(c7, 'prb')

        if sq == 0:
            dump('dbg_c', negC8[:, :], 'negC8')
        if not phase_gate('attn'):
            continue
        pcs = [(p, c) for p in range(NPAIR) for c in range(QC)]

        def av_banks(k):
            if k % 2 == 0:
                return pv, 'pv', pvo, 'pvo'
            return pq1[:, 0:512], 'pq1lo', pq1[:, 512:1024], 'pq1hi'

        def jloop(k):
            p, c = pcs[k]
            avA, avAk, avB, avBk = av_banks(k)
            jmax = 4 * c + 3
            info = {}

            def sc_mm(j):
                o = max(0, 128 * j - 512 * c)
                wdt = 512 - o
                for half in range(2):
                    h0 = 64 * half
                    pk = 'pq0lo' if half == 0 else 'pq0hi'
                    scoff = 512 * half
                    dep.rd('tensor', 'q_fm', 'k_fm')
                    dep.wr('tensor', pk)
                    sct = E('tensor', lambda e, p=p, c=c, j=j, h0=h0, o=o, wdt=wdt, scoff=scoff:
                            e.matmul(pq0[:, scoff + o:scoff + o + wdt],
                                     k_fm[h0:h0 + 64, p, 128 * j:128 * (j + 1)],
                                     q_fm[h0:h0 + 64, p, 512 * c + o:512 * c + o + wdt],
                                     start=True, stop=True), s_t)
                    dep.note_w(sct, pk)
                    info[(j, half)] = [sct, o, wdt]

            def exp_j(j):
                for half in range(2):
                    sct, o, wdt = info[(j, half)]
                    pk = 'pq0lo' if half == 0 else 'pq0hi'
                    scoff = 512 * half
                    eb = expt[2 * half + (j % 2)]
                    ebk = f'expt{2 * half + (j % 2)}'
                    wait('scalar', sct)
                    P.wait('scalar', c7, force=True)
                    dep.rd('scalar', 'negC8')
                    dep.wr('scalar', ebk)
                    ex = E('scalar', lambda e, eb=eb, scoff=scoff, o=o, wdt=wdt:
                           e.activation(eb[:, o:o + wdt], pq0[:, scoff + o:scoff + o + wdt],
                                        AF.Exp, bias=negC8[:, 0:1], scale=0.125), s_s)
                    dep.note_w(ex, ebk)
                    dep.note_r(ex, pk, 'negC8')
                    if j >= 4 * c:
                        wait('gpsimd', ex)
                        dep.rd('gpsimd', 'tables')
                        dep.wr('gpsimd', ebk)
                        ex = E('gpsimd', lambda e, eb=eb, o=o: e.tensor_tensor(
                            eb[:, o:o + 128], eb[:, o:o + 128], causal[:, :], ALU.mult), s_gp)
                        dep.note_w(ex, ebk)
                    info[(j, half)] = (ex, eb, ebk, o, wdt)

            def av(j):
                for half in range(2):
                    ex, eb, ebk, o, wdt = info[(j, half)]
                    ebs = eb[:, o:o + wdt]
                    hh = 2 * p + half
                    wait('tensor', ex)
                    dep.rd('tensor', 'v_aug')
                    dep.wr('tensor', avAk if half == 0 else avBk)
                    if half == 0:
                        tk = E('tensor', lambda e, j=j, hh=hh, ebs=ebs, o=o, wdt=wdt, st=(j == 0), sp=(j == jmax), avA=avA:
                               e.matmul(avA[0:65, o:o + wdt], v_aug[:, j, 65 * hh:65 * hh + 65],
                                        ebs, start=st, stop=sp), s_t)
                    else:
                        E('tensor', lambda e, j=j, ebs=ebs, o=o, wdt=wdt, st=(j == 0), sp=(j == jmax), avB=avB:
                          e.matmul(avB[0:1, o:o + wdt], ones_col[:, 0:1],
                                   ebs, start=st, stop=sp))
                        tk = E('tensor', lambda e, j=j, hh=hh, ebs=ebs, o=o, wdt=wdt, st=(j == 0), sp=(j == jmax), avB=avB:
                               e.matmul(avB[64:128, o:o + wdt], v_aug[:, j, 65 * hh:65 * hh + 64],
                                        ebs, start=st, stop=sp), s_t)
                    dep.note_w(tk, avAk if half == 0 else avBk)
                    dep.note_r(tk, ebk)

            for j in range(jmax + 1):
                sc_mm(j)
                exp_j(j)
                if j >= 1:
                    av(j - 1)
            av(jmax)
            av_done = (s_t, s_t.n)
            ra, rak = (rbA, 'rbA') if k % 2 == 0 else (rbA1, 'rbA1')
            rb_, rbk = (rbB, 'rbB') if k % 2 == 0 else (rbB1, 'rbB1')
            wait('vector', av_done)
            dep.wr('vector', rak, rbk)
            E('vector', lambda e, avA=avA, ra=ra: e.reciprocal(ra[64:65, :], avA[64:65, :]))
            rc = E('vector', lambda e, avB=avB, rb_=rb_: e.reciprocal(rb_[0:1, :], avB[0:1, :]), s_v)
            dep.note_w(rc, rak, rbk)
            dep.note_r(rc, avAk, avBk)
            return (p, c, avA, avAk, avB, avBk, rc, ra, rak, rb_, rbk)

        def norm(st8):
            p, c, avA, avAk, avB, avBk, rc, ra, rak, rb_, rbk = st8
            wait('tensor', rc)
            dep.rd('tensor', 'ones')
            dep.wr('tensor', 'prb')
            E('tensor', lambda e, ra=ra: e.matmul(prb[0:64, :], ones32[64:65, :], ra[64:65, :], start=True, stop=True))
            rb = E('tensor', lambda e, rb_=rb_: e.matmul(prb[64:128, :], ones32[0:1, :], rb_[0:1, :], start=True, stop=True), s_t)
            dep.note_w(rb, 'prb')
            dep.note_r(rb, rak, rbk)
            wait('scalar', rb)
            dep.wr('scalar', rak, rbk)
            E('scalar', lambda e, ra=ra: e.copy(ra[0:64, :], prb[0:64, :]))
            rbe = E('scalar', lambda e, rb_=rb_: e.copy(rb_[64:128, :], prb[64:128, :]), s_s)
            dep.note_w(rbe, rak, rbk)
            dep.note_r(rbe, 'prb')
            wait('vector', rbe)
            dep.wr('vector', f'fmc:{c}')
            E('vector', lambda e, p=p, c=c, avA=avA, ra=ra: e.tensor_tensor(
                fm[0:64, p, 512 * c:512 * (c + 1)], avA[0:64, :], ra[0:64, :], ALU.mult))
            fme = E('vector', lambda e, p=p, c=c, avB=avB, rb_=rb_: e.tensor_tensor(
                fm[64:128, p, 512 * c:512 * (c + 1)], avB[64:128, :], rb_[64:128, :], ALU.mult), s_v)
            dep.note_w(fme, f'fmc:{c}')
            dep.note_r(fme, avAk, avBk, rak, rbk)

        pend = None
        for k in range(len(pcs)):
            st8 = jloop(k)
            if pend is not None:
                norm(pend)
            pend = st8
        norm(pend)
        phase2_done = (s_t, s_t.n)
        dep.note_r(phase2_done, 'v_aug', 'q_fm', 'k_fm')
        dep.note_r((s_v, s_v.n), 'q_fm', 'k_fm')

        if sq == 0:
            dump('dbg_attn', fm[:, :, :], *[f'fmc:{c}' for c in range(4)])
        if not phase_gate('wo'):
            continue
        phase_gate('ln2')
        phase_gate('gate')

        def gate_mms(i):
            sl = 8 * (i % 4)
            dep.rd('tensor', f'fmc:{i // 4}', 'tables', 'gateT')
            dep.wr('tensor', 'prb')
            for c in range(DC):
                tk = E('tensor', lambda e, i=i, c=c, sl=sl:
                       e.matmul(prb[:, sl:sl + 8], fm[:, c, 128 * i:128 * (i + 1)],
                                gateT[:, c, 0:8], start=(c == 0), stop=(c == DC - 1)),
                       s_t if c == DC - 1 else None)
            dep.note_w(tk, 'prb')
            dep.note_r(tk, f'fmc:{i // 4}')
            return tk

        def gate_chain(g, tk):
            L = prb[:, 0:32].rearrange('p (a b) -> p a b', b=8)
            mrep = AP(g_m, 0, [[4, 128], [1, 4], [0, 8]])
            rrep = AP(g_r, 0, [[4, 128], [1, 4], [0, 8]])
            a3 = g_s[:, :].rearrange('p (a b) -> p a b', b=8)
            b3 = g_t1[:, :].rearrange('p (a b) -> p a b', b=8)
            wait('vector', tk)
            dep.wr('vector', 'g_m', 'g_s', 'g_t1', 'g_r')
            ch = []

            def step(eng, fn):
                for tt in ch[-1:]:
                    P.wait(eng, tt, force=True)
                ch.append(E(eng, fn, s_v if eng == 'vector' else s_s))
            step('vector', lambda e: e.reduce_max(g_m[:, :], L, axis=AX.X, negate=True))
            step('vector', lambda e: e.tensor_tensor(a3, L, mrep, ALU.add))
            step('scalar', lambda e: e.activation(g_t1[:, :], g_s[:, :], AF.Exp))
            step('vector', lambda e: e.reduce_sum(g_r[:, :], b3, axis=AX.X))
            step('vector', lambda e: e.reciprocal(g_r[:, :], g_r[:, :]))
            step('vector', lambda e: e.tensor_tensor(a3, b3, rrep, ALU.mult))
            step('vector', lambda e: e.reduce_max(g_m[:, :], a3, axis=AX.X))
            step('vector', lambda e: e.tensor_tensor(b3, a3, mrep, ALU.is_ge))
            step('vector', lambda e: e.tensor_scalar(g_t1[:, :], g_t1[:, :], -1.0, 1.0, ALU.mult, ALU.add))
            step('vector', lambda e: e.tensor_tensor(b3, a3, b3, ALU.mult))
            step('vector', lambda e: e.reduce_max(g_m[:, :], b3, axis=AX.X))
            step('vector', lambda e: e.tensor_tensor(b3, a3, mrep, ALU.is_ge))
            ks = [f'w_tok:{4 * g + t}' for t in range(4)]
            dep.wr('vector', *ks)
            step('vector', lambda e, g=g: e.tensor_tensor(
                w_tok[:, 4 * g:4 * (g + 1), :], a3, b3, ALU.mult))
            wt = ch[-1]
            dep.note_w(wt, *ks, 'g_s', 'g_m', 'g_t1', 'g_r')
            dep.note_r(wt, 'prb')

        pend_ln = None
        for i in range(NT):
            par = i % 2
            xr = x_load(sq, i, res[:, i, :], f'res:{i}')
            pq = pq0 if par == 0 else pq1
            pks = (f'pq{par}lo', f'pq{par}hi')
            dep.rd('tensor', 'wbufB', 'biaspg', f'fmc:{i // 4}')
            dep.wr('tensor', pks[0], pks[1])
            for p in range(NPAIR):
                for oh in range(2):
                    E('tensor', lambda e, i=i, p=p, oh=oh, pq=pq:
                      e.matmul(pq[:, 512 * oh:512 * (oh + 1)],
                               fm[:, p, 128 * i:128 * (i + 1)],
                               wbufB[:, p, 512 * oh:512 * (oh + 1)],
                               start=(p == 0), stop=False))
            for oh in range(2):
                tk = E('tensor', lambda e, oh=oh, pq=pq:
                       e.matmul(pq[:, 512 * oh:512 * (oh + 1)], ones_bf[0:1, :],
                                biaspg[0:1, 3 * DM + 512 * oh:3 * DM + 512 * (oh + 1)],
                                start=False, stop=True), s_t if oh == 1 else None)
            dep.note_w(tk, pks[0], pks[1])
            dep.note_r(tk, 'wbufB', f'fmc:{i // 4}')
            wait('vector', tk)
            wait('vector', xr)
            dep.wr('vector', f'res:{i}')
            E('vector', lambda e, i=i, pq=pq: e.tensor_tensor(
                res[:, i, 0:512], pq[:, 0:512], res[:, i, 0:512], ALU.add))
            ra = E('vector', lambda e, i=i, pq=pq: e.tensor_tensor(
                res[:, i, 512:1024], pq[:, 512:1024], res[:, i, 512:1024], ALU.add), s_v)
            dep.note_w(ra, f'res:{i}')
            dep.note_r(ra, pks[0], pks[1])
            xm, mbuf, mk = ln_A(i, res[:, i, :], (f'res:{i}',), par)
            vv = ln_S(i, par, xm, mbuf, mk)
            if pend_ln is not None:
                pi = pend_ln[0]
                ln_B(*pend_ln)
                gtk = gate_mms(pi)
                if pi % 4 == 3:
                    gate_chain(pi // 4, gtk)
            pend_ln = (i, par, vv, mbuf, mk, f'fmc:{i // 4}')
        ln_B(*pend_ln)
        gtk = gate_mms(NT - 1)
        gate_chain(NT // 4 - 1, gtk)

        for b in range(2):
            dep.rd('tensor', 'tables')
            dep.wr('tensor', 'pt0', 'pt1')
            for t8 in range(8):
                i = 8 * b + t8
                dep.rd('tensor', f'w_tok:{i}')
                tk = E('tensor', lambda e, i=i, t8=t8: e.transpose(
                    pt[0:8, 128 * t8:128 * (t8 + 1)], w_tok[:, i, :], ident_bf[:, :]),
                    s_t if t8 == 7 else None)
            dep.note_w(tk, 'pt0', 'pt1')
            for t8 in range(8):
                dep.note_r(tk, f'w_tok:{8 * b + t8}')
            wait('vector', tk)
            dep.wr('vector', 'w_fm')
            wf = E('vector', lambda e, b=b: e.tensor_copy(
                w_fm[0:8, 1024 * b:1024 * (b + 1)], pt[0:8, 0:1024]), s_v)
            dep.note_r(wf, 'pt0', 'pt1')
            dep.note_w(wf, 'w_fm')

        if sq == 0:
            dump('dbg_w', w_tok[:, :, :], *[f'w_tok:{i}' for i in range(NT)])
            dump('dbg_fm2', fm[:, :, :], *[f'fmc:{c}' for c in range(4)])
        if not phase_gate('ffn'):
            continue

        def wrep_for(e_n):
            dep.rd('tensor', 'w_fm', 'tables')
            WRB = [(pv[:, 0:512], 'pv'), (pvo[:, 0:512], 'pvo'),
                   (pq1[:, 0:512], 'pq1lo'), (pq1[:, 512:1024], 'pq1hi')]
            for cc in range(4):
                dep.wr('tensor', WRB[cc][1])
            for cc in range(4):
                bc = E('tensor', lambda e, e_n=e_n, cc=cc, pz=WRB[cc][0]:
                       e.matmul(pz[:, :], sel[:, 128 * e_n:128 * (e_n + 1)],
                                w_fm[0:8, 512 * cc:512 * (cc + 1)], start=True, stop=True),
                       s_t if cc == 3 else None)
            for cc in range(4):
                dep.note_w(bc, WRB[cc][1])
            dep.note_r(bc, 'w_fm')
            wait('vector', bc)
            dep.wr('vector', 'wrep')
            for cc in range(4):
                wr_t = E('vector', lambda e, cc=cc, pz=WRB[cc][0]:
                         e.tensor_copy(wrep[:, 512 * cc:512 * (cc + 1)], pz[:, :]),
                         s_v if cc == 3 else None)
            dep.note_w(wr_t, 'wrep')
            for cc in range(4):
                dep.note_r(wr_t, WRB[cc][1])

        wrep_for(0)
        fill(w1bufA, 'w1bufA', w1_slices(0, 1), s_wa)
        fill(w1bufB, 'w1bufB', w1_slices(0, 0), s_wb)
        fill(w2bufA, 'w2bufA', w2_slices(0), s_w2a)
        for e_i in range(NEXP):
            routed = e_i < 8
            if e_i % 2 == 0:
                w1g_buf, w1g_key, s_g = w1bufA, 'w1bufA', s_wa
                w1y_buf, w1y_key, s_y = w1bufB, 'w1bufB', s_wb
                w2_buf, w2_key = w2bufA, 'w2bufA'
                w2n_buf, w2n_key, s_w2n = w2bufB, 'w2bufB', s_w2b
            else:
                w1g_buf, w1g_key, s_g = w1bufB, 'w1bufB', s_wb
                w1y_buf, w1y_key, s_y = w1bufA, 'w1bufA', s_wa
                w2_buf, w2_key = w2bufB, 'w2bufB'
                w2n_buf, w2n_key, s_w2n = w2bufA, 'w2bufA', s_w2a
            if debug and sq == 0 and e_i == 1:
                dump('dbg_w1A', w1bufA[:, :, :], 'w1bufA')
                dump('dbg_w1B', w1bufB[:, :, :], 'w1bufB')
                dump('dbg_w2', w2_buf[:, :, :], w2_key)
            W1BANKS = [(pq0, 0, 'pq0lo'), (pq0, 512, 'pq0hi'), (pq1, 0, 'pq1lo'), (pq1, 512, 'pq1hi')]
            for j in range(NJ):
                si_last = None
                for htc in range(2):
                    tcs = (2 * htc, 2 * htc + 1)
                    dep.rd('tensor', w1g_key, f'fmc:{tcs[0]}', f'fmc:{tcs[1]}')
                    for tc in tcs:
                        dep.wr('tensor', W1BANKS[tc][2])
                    for cch in range(DC):
                        for tc in tcs:
                            psc, sl, pk = W1BANKS[tc]
                            tk = E('tensor', lambda e, j=j, tc=tc, cch=cch, psc=psc, sl=sl, wb=w1g_buf:
                                   e.matmul(psc[:, sl:sl + 512], wb[:, cch, 128 * j:128 * (j + 1)],
                                            fm[:, cch, 512 * tc:512 * (tc + 1)],
                                            start=(cch == 0), stop=(cch == DC - 1)),
                                   s_t if (cch == DC - 1 and tc == tcs[1]) else None)
                    for tc in tcs:
                        psc, sl, pk = W1BANKS[tc]
                        dep.note_w(tk, pk)
                        dep.note_r(tk, w1g_key, f'fmc:{tc}')
                    wait('scalar', tk)
                    for tc in tcs:
                        psc, sl, pk = W1BANKS[tc]
                        dep.wr('scalar', f'act:{j}')
                        si = E('scalar', lambda e, j=j, tc=tc, psc=psc, sl=sl:
                               e.activation(act[:, j, 512 * tc:512 * (tc + 1)], psc[:, sl:sl + 512], AF.Silu), s_s)
                        dep.note_w(si, f'act:{j}')
                        dep.note_r(si, pk)
                        si_last = si
                if routed:
                    wait('vector', si_last)
                    dep.rd('vector', 'wrep')
                    dep.wr('vector', f'act:{j}')
                    aw = E('vector', lambda e, j=j: e.tensor_tensor(
                        act[:, j, :], act[:, j, :], wrep[:, :], ALU.mult), s_v)
                    dep.note_w(aw, f'act:{j}')
                    dep.note_r(aw, 'wrep')
            if e_i + 1 < NEXP:
                fill(w1g_buf, w1g_key, w1_slices(e_i + 1, 0), s_g)
            for j in range(NJ):
                for htc in range(2):
                    tcs = (2 * htc, 2 * htc + 1)
                    dep.rd('tensor', w1y_key, f'fmc:{tcs[0]}', f'fmc:{tcs[1]}')
                    for tc in tcs:
                        dep.wr('tensor', W1BANKS[tc][2])
                    for cch in range(DC):
                        for tc in tcs:
                            psc, sl, pk = W1BANKS[tc]
                            tk = E('tensor', lambda e, j=j, tc=tc, cch=cch, psc=psc, sl=sl, wb=w1y_buf:
                                   e.matmul(psc[:, sl:sl + 512], wb[:, cch, 128 * j:128 * (j + 1)],
                                            fm[:, cch, 512 * tc:512 * (tc + 1)],
                                            start=(cch == 0), stop=(cch == DC - 1)),
                                   s_t if (cch == DC - 1 and tc == tcs[1]) else None)
                    for tc in tcs:
                        psc, sl, pk = W1BANKS[tc]
                        dep.note_w(tk, pk)
                        dep.note_r(tk, w1y_key, f'fmc:{tc}')
                    wait('vector', tk)
                    for tc in tcs:
                        psc, sl, pk = W1BANKS[tc]
                        dep.wr('vector', f'act:{j}')
                        ya = E('vector', lambda e, j=j, tc=tc, psc=psc, sl=sl:
                               e.tensor_tensor(act[:, j, 512 * tc:512 * (tc + 1)], psc[:, sl:sl + 512],
                                               act[:, j, 512 * tc:512 * (tc + 1)], ALU.mult), s_v)
                        dep.note_w(ya, f'act:{j}')
                        dep.note_r(ya, pk)
            if debug and sq == 0 and e_i == 1:
                dump('dbg_act0', act[:, :, :], *[f'act:{j}' for j in range(NJ)])
            if e_i + 1 < NEXP:
                fill(w1y_buf, w1y_key, w1_slices(e_i + 1, 1), s_y)
                fill(w2n_buf, w2n_key, w2_slices(e_i + 1), s_w2a if w2n_key == 'w2bufA' else s_w2b)
                if e_i + 1 < 8:
                    wrep_for(e_i + 1)
            for i in range(NT):
                if i % 2 == 0:
                    zb = [(pv[:, 0:512], 'pv'), (pvo[:, 0:512], 'pvo')]
                else:
                    zb = [(pq1[:, 0:512], 'pq1lo'), (pq1[:, 512:1024], 'pq1hi')]
                dep.rd('tensor', w2_key, *[f'act:{j}' for j in range(NJ)])
                dep.wr('tensor', zb[0][1], zb[1][1])
                for j in range(NJ):
                    for oh in range(2):
                        tk = E('tensor', lambda e, i=i, oh=oh, j=j, pz=zb[oh][0], wb=w2_buf:
                               e.matmul(pz[:, :], act[:, j, 128 * i:128 * (i + 1)],
                                        wb[:, j, 512 * oh:512 * (oh + 1)],
                                        start=(j == 0), stop=(j == NJ - 1)),
                               s_t if (j == NJ - 1 and oh == 1) else None)
                dep.note_w(tk, zb[0][1], zb[1][1])
                dep.note_r(tk, w2_key, *[f'act:{j}' for j in range(NJ)])
                wait('vector', tk)
                dep.wr('vector', f'res:{i}')
                for oh in range(2):
                    za = E('vector', lambda e, i=i, oh=oh, pz=zb[oh][0]: e.tensor_tensor(
                        res[:, i, 512 * oh:512 * (oh + 1)], pz[:, :],
                        res[:, i, 512 * oh:512 * (oh + 1)], ALU.add), s_v if oh == 1 else None)
                za = (s_v, s_v.n)
                dep.note_w(za, f'res:{i}')
                dep.note_r(za, zb[0][1], zb[1][1])
                if e_i == NEXP - 1:
                    wait('gpsimd', za)
                    s_out.tok(16)
                    em('gpsimd', lambda e, sq=sq, i=i, h=s_out.h:
                       e.dma_start(dout[sq, 128 * i:128 * (i + 1), :], res[:, i, :]).then_inc(h, 16))
        out_tot = (s_out, s_out.n)
        for i in range(NT):
            dep.note_r(out_tot, f'res:{i}')
        dep.note_r(out_tot, 'q_fm')
        dep.note_r(out_tot, 'k_fm')
        if sq + 1 < nseq:
            fill(wbufA, 'wbufA', qkv_slices(swq), s_wa)

    if s_out.n == 0:
        dep.wr('vector', 'xs0')
        mz = E('vector', lambda e: e.memset(xslot[0][:, :], 0.0), s_v)
        dep.note_w(mz, 'xs0')
        wait('gpsimd', mz)
        s_out.tok(16)
        em('gpsimd', lambda e, h=s_out.h: e.dma_start(dout[0, 0:128, :], xslot[0][:, :]).then_inc(h, 16))
    em('gpsimd', lambda e, h=s_out.h, v=s_out.n: e.wait_ge(h, v))
    if debug and s_dbg.n:
        em('gpsimd', lambda e, h=s_dbg.h, v=s_dbg.n: e.wait_ge(h, v))

    with nc.Block() as block:
        for name in ENGS:
            lst = P.ops[name]
            getattr(block, name)(lambda eng, lst=lst: [f(eng) for f in lst])

    stack.close()
    nc.finalize()
    return nc


_CACHE = {}


def _get_nc(nseq=NSEQ):
    if nseq not in _CACHE:
        _CACHE[nseq] = build_program(nseq)
    return _CACHE[nseq]


def make_tables():
    dh = DH
    inv_freq = (1.0 / (np.float32(10000.0) ** (np.arange(0, dh, 2, dtype=np.float32) / np.float32(dh)))).astype(np.float32)
    t = np.arange(S, dtype=np.float32)
    freqs = (t[:, None] * inv_freq[None, :]).astype(np.float32)
    emb = np.concatenate([freqs, freqs], axis=-1)
    return dict(
        cos_t=np.cos(emb).astype(np.float32),
        sin_t=np.sin(emb).astype(np.float32),
        ident_bf=np.eye(128, dtype=ml_dtypes.bfloat16),
        causal_bf=np.triu(np.ones((128, 128), dtype=np.float32)).astype(ml_dtypes.bfloat16),
        sel_bf=_sel(),
    )


def _sel():
    sel = np.zeros((8, 1024), dtype=ml_dtypes.bfloat16)
    for e in range(8):
        sel[e, 128 * e:128 * (e + 1)] = 1.0
    return sel


def make_in_map(x_shard, W):
    m = dict(W)
    m['x'] = x_shard
    return m


def kernel(L0_attn_x, L0_attn_Wq, L0_attn_bq, L0_attn_Wk, L0_attn_bk,
           L0_attn_Wv, L0_attn_bv, L0_attn_Wo, L0_attn_bo,
           L0_ffn_gate_w, L0_ffn_W1, L0_ffn_W2, L0_ffn_Ws1, L0_ffn_Ws2):
    x = np.ascontiguousarray(np.asarray(L0_attn_x, dtype=np.float32))
    per = x.shape[0] // NCORES
    assert per == NSEQ
    nc = _get_nc(NSEQ)
    W = dict(
        wq=np.ascontiguousarray(L0_attn_Wq, dtype=np.float32),
        wk=np.ascontiguousarray(L0_attn_Wk, dtype=np.float32),
        wv=np.ascontiguousarray(L0_attn_Wv, dtype=np.float32),
        wo=np.ascontiguousarray(L0_attn_Wo, dtype=np.float32),
        bq=np.ascontiguousarray(L0_attn_bq, dtype=np.float32),
        bk=np.ascontiguousarray(L0_attn_bk, dtype=np.float32),
        bv=np.ascontiguousarray(L0_attn_bv, dtype=np.float32),
        bo=np.ascontiguousarray(L0_attn_bo, dtype=np.float32),
        gate_w=np.ascontiguousarray(L0_ffn_gate_w, dtype=np.float32),
        w1=np.ascontiguousarray(L0_ffn_W1, dtype=np.float32),
        w2=np.ascontiguousarray(L0_ffn_W2, dtype=np.float32),
        ws1=np.ascontiguousarray(L0_ffn_Ws1, dtype=np.float32),
        ws2=np.ascontiguousarray(L0_ffn_Ws2, dtype=np.float32),
        **make_tables(),
    )
    in_maps = [make_in_map(x[per * i:per * (i + 1)], W) for i in range(NCORES)]
    resu = run_bass_kernel_spmd(nc, in_maps, core_ids=list(range(NCORES)))
    return np.concatenate([np.asarray(r['out']) for r in resu.results], axis=0)


# version 17 (final)
# speedup vs baseline: 1.2006x; 1.0810x over previous
import contextlib
import numpy as np
import ml_dtypes
import concourse.bass as bass
import concourse.mybir as mybir
from concourse.bass_utils import run_bass_kernel_spmd

F32 = mybir.dt.float32
BF16 = mybir.dt.bfloat16
AF = mybir.ActivationFunctionType
ALU = mybir.AluOpType
AX = mybir.AxisListType
AP = bass.AP

NCORES = 8
NSEQ = 4
S = 2048
DM = 1024
NT = 16
NH = 16
NPAIR = 8
DH = 64
DC = 8
QC = 4
NEXP = 10
HE = 768
NJ = 6
EPS = 1e-5

ENGS = ('sync', 'gpsimd', 'scalar', 'vector', 'tensor')

ALIAS = {
    'wbufA': ['w1bufA', 'w1bufB', 'rbA1', 'rbB1'],
    'wbufB': ['w1bufB'],
    'w1bufA': ['wbufA', 'rbA1', 'rbB1'],
    'w1bufB': ['wbufA', 'wbufB'],
    'w2bufA': ['xs0', 'xs1', 'msc1', 'lnbf2'],
    'w2bufB': ['msc2', 'lnbf', 'expt0', 'expt1', 'expt2', 'expt3', 'rbA', 'rbB'],
    'xs0': ['w2bufA', 'lnbf2'], 'xs1': ['w2bufA', 'qsbb'], 'msc1': ['w2bufA', 'msc1b'],
    'lnbf2': ['w2bufA', 'xs0'],
    'msc2': ['w2bufB', 'msc2b'], 'msc2b': ['w2bufB', 'msc2'], 'msc1b': ['w2bufA', 'msc1'], 'qsbb': ['w2bufA', 'xs1'], 'lnbf': ['w2bufB', 'rbB'],
    'expt0': ['w2bufB', 'gnat'], 'expt1': ['w2bufB', 'gnat'], 'expt2': ['w2bufB'], 'expt3': ['w2bufB'],
    'rbA': ['w2bufB'], 'rbB': ['lnbf', 'w2bufB'],
    'rbA1': ['wbufA', 'w1bufA'], 'rbB1': ['wbufA', 'w1bufA'],
    'gnat': ['expt0', 'expt1', 'w2bufB'],
    'q_fm': [f'res:{i}' for i in range(8)],
    'k_fm': [f'res:{i}' for i in range(8, 16)],
    'v_aug': [f'act:{j}' for j in range(NJ)] + ['wrep', 'w_fm', 'dumpb'],
    'dumpb': ['v_aug', 'act:0'],
    'wrep': ['v_aug'], 'w_fm': ['v_aug'],
}
for i in range(8):
    ALIAS[f'res:{i}'] = ['q_fm']
for i in range(8, 16):
    ALIAS[f'res:{i}'] = ['k_fm']
for j in range(NJ):
    ALIAS[f'act:{j}'] = ['v_aug']
ALIAS['act:0'].append('dumpb')


class Sem:
    def __init__(self, handle, name):
        self.h = handle
        self.name = name
        self.n = 0

    def tok(self, k=1):
        self.n += k
        return (self, self.n)


class Prog:
    def __init__(self):
        self.ops = {e: [] for e in ENGS}
        self.waited = {e: {} for e in ENGS}
        self.own = {}

    def em(self, eng, fn):
        self.ops[eng].append(fn)

    def wait(self, eng, tok, force=False):
        if tok is None:
            return
        sem, val = tok
        assert val <= sem.n, f"forward wait {sem.name}:{val}>{sem.n}"

        w = self.waited[eng]
        if w.get(sem.name, -1) >= val:
            return
        w[sem.name] = val
        self.em(eng, lambda e, h=sem.h, v=val: e.wait_ge(h, v))


class Deps:

    def __init__(self, prog):
        self.p = prog
        self.d = {}

    def _expand(self, keys):
        out = list(keys)
        for k in keys:
            out.extend(ALIAS.get(k, ()))
        return out

    def rd(self, eng, *keys):
        for k in keys:
            e = self.d.get(k)
            if e:
                for t in e['w']:
                    self.p.wait(eng, t)

    def wr(self, eng, *keys):
        for k in self._expand(keys):
            e = self.d.get(k)
            if e:
                for t in e['w']:
                    self.p.wait(eng, t)
                for t in e['r']:
                    self.p.wait(eng, t)

    def note_w(self, tok, *keys):
        for k in keys:
            e = self.d.setdefault(k, {'w': [], 'r': []})
            e['w'].append(tok)
            if len(e['w']) > 24:
                e['w'] = e['w'][-24:]
            e['r'] = []

    def note_r(self, tok, *keys):
        for k in keys:
            e = self.d.setdefault(k, {'w': [], 'r': []})
            e['r'].append(tok)


def build_program(nseq=NSEQ, debug=False, stop_after=None, ln1_tiles=NT, ln_steps=99, pj_steps=99, skip_casts=False, skip_tab=False, skip_gate=False):
    nc = bass.Bass("TRN2", target_bir_lowering=False)

    dx = nc.dram_tensor('x', [nseq, S, DM], F32, kind='ExternalInput')
    dwq = nc.dram_tensor('wq', [DM, DM], F32, kind='ExternalInput')
    dwk = nc.dram_tensor('wk', [DM, DM], F32, kind='ExternalInput')
    dwv = nc.dram_tensor('wv', [DM, DM], F32, kind='ExternalInput')
    dwo = nc.dram_tensor('wo', [DM, DM], F32, kind='ExternalInput')
    dbq = nc.dram_tensor('bq', [DM], F32, kind='ExternalInput')
    dbk = nc.dram_tensor('bk', [DM], F32, kind='ExternalInput')
    dbv = nc.dram_tensor('bv', [DM], F32, kind='ExternalInput')
    dbo = nc.dram_tensor('bo', [DM], F32, kind='ExternalInput')
    dgate = nc.dram_tensor('gate_w', [8, DM], F32, kind='ExternalInput')
    dw1 = nc.dram_tensor('w1', [8, 2 * HE, DM], F32, kind='ExternalInput')
    dw2 = nc.dram_tensor('w2', [8, DM, HE], F32, kind='ExternalInput')
    dws1 = nc.dram_tensor('ws1', [4 * HE, DM], F32, kind='ExternalInput')
    dws2 = nc.dram_tensor('ws2', [DM, 2 * HE], F32, kind='ExternalInput')
    dcos = nc.dram_tensor('cos_t', [S, DH], F32, kind='ExternalInput')
    dsin = nc.dram_tensor('sin_t', [S, DH], F32, kind='ExternalInput')
    dident_bf = nc.dram_tensor('ident_bf', [128, 128], BF16, kind='ExternalInput')
    dcausal = nc.dram_tensor('causal_bf', [128, 128], BF16, kind='ExternalInput')
    dsel = nc.dram_tensor('sel_bf', [8, 1024], BF16, kind='ExternalInput')
    dout = nc.dram_tensor('out', [nseq, S, DM], F32, kind='ExternalOutput')
    if debug:
        ddbg = {k: nc.dram_tensor(k, shp, F32, kind='ExternalOutput') for k, shp in [
            ('dbg_fm1', [128, DC, S]), ('dbg_q', [128, NPAIR, S]), ('dbg_k', [128, NPAIR, S]),
            ('dbg_v', [128, NT, NH * 65]), ('dbg_c', [128, 1]), ('dbg_attn', [128, DC, S]),
            ('dbg_res1', [128, NT, DM]), ('dbg_w', [128, NT, 8]), ('dbg_fm2', [128, DC, S]),
            ('dbg_act0', [128, NJ, S]), ('dbg_w1A', [128, DC, HE]), ('dbg_w1B', [128, DC, HE]), ('dbg_w2', [128, NJ, DM])]}

    swq = nc.dram_tensor('swq', [DM, DM], BF16, kind='Internal')
    swk = nc.dram_tensor('swk', [DM, DM], BF16, kind='Internal')
    swv = nc.dram_tensor('swv', [DM, DM], BF16, kind='Internal')
    swo = nc.dram_tensor('swo', [DM, DM], BF16, kind='Internal')
    sw1 = nc.dram_tensor('sw1', [8, 2 * HE, DM], BF16, kind='Internal')
    sw2 = nc.dram_tensor('sw2', [8, DM, HE], BF16, kind='Internal')
    sws1 = nc.dram_tensor('sws1', [4 * HE, DM], BF16, kind='Internal')
    sws2 = nc.dram_tensor('sws2', [DM, 2 * HE], BF16, kind='Internal')
    sgate = nc.dram_tensor('sgate', [8, DM], BF16, kind='Internal')

    BASE = (nc.sbuf_base + 63) // 64 * 64

    def sb(name, shape, dtype, at):
        return nc.alloc_sbuf_tensor_at(name, shape, dtype, offset=BASE + at)

    res = sb('res', [128, NT, DM], F32, 0)
    q_fm = sb('q_fm', [128, NPAIR, S], BF16, 0)
    k_fm = sb('k_fm', [128, NPAIR, S], BF16, 32768)
    fm = sb('fm', [128, DC, S], BF16, 65536)
    VB = 98304
    v_aug = sb('v_aug', [128, NT, NH * 65], BF16, VB)
    dumpb = sb('dumpb', [128, DM], BF16, VB)
    act = sb('act', [128, NJ, S], BF16, VB)
    wrep = sb('wrep', [128, S], BF16, VB + 24576)
    w_fm = sb('w_fm', [8, S], BF16, VB + 28672)
    WB = 131584
    wbufA = sb('wbufA', [128, DC, DM], BF16, WB)
    wbufB = sb('wbufB', [128, DC, DM], BF16, WB + 16384)
    rbA1 = sb('rbA1', [128, 512], F32, WB)
    rbB1 = sb('rbB1', [128, 512], F32, WB + 2048)
    w1bufA = sb('w1bufA', [128, DC, HE], BF16, WB)
    w1bufB = sb('w1bufB', [128, DC, HE], BF16, WB + 12288)
    PB = 164352
    xslot = [sb('xslot0', [128, DM], F32, PB),
             sb('xslot1', [128, DM], F32, PB + 4096)]
    msc1 = sb('msc1', [128, DM], F32, PB + 8192)
    msc2 = sb('msc2', [128, DM], F32, PB + 12288)
    msc1b = sb('msc1b', [128, DM], BF16, PB + 8192)
    msc2b = sb('msc2b', [128, DM], BF16, PB + 12288)
    qsbb = sb('qsbb', [128, DM], BF16, PB + 4096)
    lnbf = sb('lnbf', [128, DM], BF16, PB + 16384)
    lnbf2 = sb('lnbf2', [128, DM], BF16, PB)
    expt = [sb(f'expt{i}', [128, 512], BF16, PB + 18432 + 1024 * i) for i in range(4)]
    gnat = sb('gnat', [8, 1024], BF16, PB + 18432)
    rbB = sb('rbB', [128, 512], F32, PB + 16384)
    rbA = sb('rbA', [128, 512], F32, PB + 22528)
    w2bufA = sb('w2bufA', [128, NJ, DM], BF16, PB)
    w2bufB = sb('w2bufB', [128, NJ, DM], BF16, PB + 12288)
    off = PB + 24576
    biaspg = sb('biaspg', [1, 4 * DM], BF16, off); off += 8192
    cosr = sb('cosr', [128, NT, DH], BF16, off); off += 2048
    sinr = sb('sinr', [128, NT, DH], BF16, off); off += 2048
    ident_bf = sb('ident_bf', [128, 128], BF16, off); off += 256
    causal = sb('causal', [128, 128], BF16, off); off += 256
    sel = sb('sel', [8, 1024], BF16, off); off += 2048
    gateT = sb('gateT', [128, DC, 16], BF16, off); off += 512
    ones_bf = sb('ones_bf', [1, 128], BF16, off); off += 256
    ones_col = sb('ones_col', [128, 1], BF16, off); off += 32
    ones32 = sb('ones32', [128, 64], F32, off); off += 256
    stat_nm = sb('stat_nm', [128, 2], F32, off); off += 32
    stat_rs = sb('stat_rs', [128, 2], F32, off); off += 32
    stat_v1 = sb('stat_v1', [128, 4], F32, off); off += 32
    qn2 = sb('qn2', [128, NT], F32, off); off += 64
    qn2m = sb('qn2m', [128, 1], BF16, off); off += 32
    kn2m = sb('kn2m', [128, 1], BF16, off); off += 32
    cnst = sb('cnst', [1, 16], F32, off); off += 64
    negC8 = sb('negC8', [128, 1], F32, off); off += 32
    g_s = sb('g_s', [128, 32], F32, off); off += 128
    g_t1 = sb('g_t1', [128, 32], F32, off); off += 128
    g_m = sb('g_m', [128, 4], F32, off); off += 32
    g_r = sb('g_r', [128, 4], F32, off); off += 32
    w_tok = sb('w_tok', [128, NT, 8], BF16, off); off += 256
    assert BASE + off <= 229000, (BASE, off)
    nc.sbuf_base = BASE + off

    pq0 = nc.alloc_psum_tensor('pq0', [128, 1024], F32)
    pq1 = nc.alloc_psum_tensor('pq1', [128, 1024], F32)
    pv = nc.alloc_psum_tensor('pv', [128, 512], F32)
    pvo = nc.alloc_psum_tensor('pvo', [128, 512], F32)
    pt = nc.alloc_psum_tensor('pt', [128, 1024], BF16)
    prb = nc.alloc_psum_tensor('prb', [128, 512], F32)

    P = Prog()
    dep = Deps(P)
    stack = contextlib.ExitStack()

    def sem(name):
        return Sem(stack.enter_context(nc.semaphore(name)), name)

    s_scr = sem('s_scr')
    s_x = [sem('s_x0'), sem('s_x1')]
    s_out = sem('s_out')
    s_tab = sem('s_tab')
    s_wa = sem('s_wa')
    s_wb = sem('s_wb')
    s_w2a = sem('s_w2a')
    s_w2b = sem('s_w2b')
    s_dbg = sem('s_dbg')
    s_t = sem('s_t')
    s_v = sem('s_v')
    s_s = sem('s_s')
    s_g = sem('s_g')
    P.own = {'tensor': s_t, 'vector': s_v, 'scalar': s_s, 'gpsimd': s_g}

    em, wait = P.em, P.wait

    def E(eng, fn, inc=None):
        if inc is None:
            em(eng, fn)
            return None
        t = inc.tok()
        em(eng, lambda e, fn=fn, h=inc.h: fn(e).then_inc(h, 1))
        return t

    def gp_dma(dst, src, semx):
        t = semx.tok(16)
        em('gpsimd', lambda e, d=dst, s=src, h=semx.h: e.dma_start(d, s).then_inc(h, 16))
        return t

    if not skip_casts:
        gp_dma(swq[:, :], dwq[:, :], s_scr)
        gp_dma(swk[:, :], dwk[:, :], s_scr)
        gp_dma(swv[:, :], dwv[:, :], s_scr)
        gp_dma(swo[:, :], dwo[:, :], s_scr)
        gp_dma(sw1[:, :, :], dw1[:, :, :], s_scr)
        gp_dma(sw2[:, :, :], dw2[:, :, :], s_scr)
        gp_dma(sws1[:, :], dws1[:, :], s_scr)
        gp_dma(sws2[:, :], dws2[:, :], s_scr)
    gp_dma(sgate[:, :], dgate[:, :], s_scr)
    gp_dma(biaspg[0:1, 0:DM], dbq.ap().rearrange('(a b) -> a b', a=1), s_scr)
    gp_dma(biaspg[0:1, DM:2 * DM], dbk.ap().rearrange('(a b) -> a b', a=1), s_scr)
    gp_dma(biaspg[0:1, 2 * DM:3 * DM], dbv.ap().rearrange('(a b) -> a b', a=1), s_scr)
    gp_dma(biaspg[0:1, 3 * DM:4 * DM], dbo.ap().rearrange('(a b) -> a b', a=1), s_scr)
    gp_dma(gnat[:, :], dgate[:, :], s_scr)
    gp_dma(cosr[:, :, :], AP(dcos, 0, [[DH, 128], [128 * DH, NT], [1, DH]]), s_scr)
    gp_dma(sinr[:, :, :], AP(dsin, 0, [[DH, 128], [128 * DH, NT], [1, DH]]), s_scr)
    scr_done = (s_scr, s_scr.n)
    dep.note_w(scr_done, 'biaspg', 'gnat')

    def sy_dma(dst, src, semx, transpose=False):
        t = semx.tok(16)
        em('sync', lambda e, d=dst, s=src, h=semx.h, tr=transpose:
           e.dma_start(d, s, transpose=tr).then_inc(h, 16))
        return t

    if not skip_tab:
        sy_dma(causal[:, :], dcausal[:, :], s_tab)
        sy_dma(sel[:, :], dsel[:, :], s_tab)
    sy_dma(ident_bf[:, :], dident_bf[:, :], s_tab)
    tab_done = (s_tab, s_tab.n)
    dep.note_w(tab_done, 'tables')
    dep.note_w(scr_done, 'tables')
    if not skip_gate:
        wait('tensor', scr_done)
        wait('tensor', tab_done)
        dep.wr('tensor', 'pt0', 'pt1')
        for c in range(DC):
            gt = E('tensor', lambda e, c=c: e.transpose(pt[:, 8 * c:8 * (c + 1)], gnat[0:8, 128 * c:128 * (c + 1)], ident_bf[0:8, 0:8]),
                   s_t if c == DC - 1 else None)
        dep.note_w(gt, 'pt0')
        dep.note_r(gt, 'gnat')
        wait('vector', gt)
        gte = E('vector', lambda e: e.tensor_copy(
            AP(gateT, 0, [[DC * 16, 128], [16, DC], [1, 8]]),
            pt[:, 0:64].rearrange('p (a b) -> p a b', b=8)), s_v)
        dep.note_w(gte, 'gateT')
        dep.note_r(gte, 'pt0')

    E('vector', lambda e: e.memset(ones_bf[:, :], 1.0))
    E('vector', lambda e: e.memset(ones_col[:, :], 1.0))
    t_ones = E('vector', lambda e: e.memset(ones32[:, :], 1.0), s_v)
    dep.note_w(t_ones, 'ones')

    def fill(dst_sb, key, src_slices, semx):
        dep.wr('sync', key)
        wait('sync', scr_done)
        for c, sap in enumerate(src_slices):
            n = sap.shape[0]
            sy_dma(dst_sb[:, c, 0:n], sap, semx, transpose=True)
        tok = (semx, semx.n)
        dep.note_w(tok, key)
        return tok

    def qkv_slices(scr):
        return [scr[0:DM, 128 * c:128 * (c + 1)] for c in range(DC)]

    def w1_slices(eidx, half):
        out = []
        for c in range(DC):
            if eidx < 8:
                out.append(sw1[eidx, half * HE:(half + 1) * HE, 128 * c:128 * (c + 1)])
            else:
                r0 = half * 2 * HE + (eidx - 8) * HE
                out.append(sws1[r0:r0 + HE, 128 * c:128 * (c + 1)])
        return out

    def w2_slices(eidx):
        out = []
        for j in range(NJ):
            if eidx < 8:
                out.append(sw2[eidx, 0:DM, 128 * j:128 * (j + 1)])
            else:
                c0 = (eidx - 8) * HE + 128 * j
                out.append(sws2[0:DM, c0:c0 + 128])
        return out

    def x_load(sq, i, dst, dkey):
        par = i % 2
        sx = s_x[par]
        dep.wr('gpsimd', dkey)
        wait('gpsimd', (sx, sx.n) if sx.n else None)
        t = sx.tok(16)
        em('gpsimd', lambda e, sq=sq, i=i, d=dst, h=sx.h:
           e.dma_start(d, dx[sq, 128 * i:128 * (i + 1), :]).then_inc(h, 16))
        dep.note_w(t, dkey)
        return t

    def transpose_to(dst, dsti_off, src_tile, key_out, no_evac=False, src_key='lnbf'):
        dep.rd('tensor', 'tables')
        dep.wr('tensor', 'pt0', 'pt1')
        for cc in range(8):
            fn = lambda e, cc=cc, st=src_tile: e.transpose(
                pt[:, 128 * cc:128 * (cc + 1)],
                st[:, 128 * cc:128 * (cc + 1)], ident_bf[:, :])
            tk = E('tensor', fn, s_t if cc == 7 else None)
        dep.note_w(tk, 'pt0', 'pt1')
        dep.note_r(tk, src_key)
        if no_evac:
            return tk
        wait('vector', tk)
        dep.wr('vector', key_out)
        ev = E('vector', lambda e, o=dsti_off, d=dst: e.tensor_copy(
            AP(d, o, [[DC * S, 128], [S, DC], [1, 128]]),
            pt[:, :].rearrange('p (a b) -> p a b', b=128)), s_v)
        dep.note_r(ev, 'pt0', 'pt1')
        dep.note_w(ev, key_out)
        return ev

    def ln_A(i, src, sk, par):
        stn = stat_nm[:, par:par + 1]
        mbuf = msc1 if par == 0 else msc2
        mk = 'msc1' if par == 0 else 'msc2'
        dep.rd('vector', *sk)
        dep.wr('vector', f'stat_nm{par}')
        t = E('vector', lambda e, s=src, stn=stn: e.reduce_sum(stn, s, axis=AX.X, negate=True), s_v)
        P.wait('vector', t, force=True)
        nm = E('vector', lambda e, stn=stn: e.tensor_scalar_mul(stn, stn, 1.0 / DM), s_v)
        dep.note_w(nm, f'stat_nm{par}')
        P.wait('vector', nm, force=True)
        dep.wr('vector', mk)
        xm = E('vector', lambda e, s=src, stn=stn, mbuf=mbuf: e.tensor_scalar_add(mbuf[:, :], s, stn), s_v)
        dep.note_w(xm, mk)
        dep.note_r(xm, *sk, f'stat_nm{par}')
        return xm, mbuf, mk

    def ln_S(i, par, xm, mbuf, mk):
        wait('scalar', xm)
        dep.wr('scalar', 'dumpb', f'stat_v1{par}')
        vv = E('scalar', lambda e, mbuf=mbuf, par=par: e.activation(
            dumpb[:, :], mbuf[:, :], AF.Square, accum_out=stat_v1[:, par:par + 1]), s_s)
        dep.note_w(vv, 'dumpb', f'stat_v1{par}')
        dep.note_r(vv, mk)
        return vv

    def ln_B(i, par, vv, mbuf, mk, key_out):
        srs = stat_rs[:, par:par + 1]
        sv1 = stat_v1[:, par:par + 1]
        lnb = lnbf if par == 0 else lnbf2
        lbk = 'lnbf' if par == 0 else 'lnbf2'
        wait('vector', vv)
        dep.wr('vector', f'stat_rs{par}', f'stat_v1{par}')
        t = E('vector', lambda e, sv1=sv1: e.tensor_scalar(sv1, sv1, 1.0 / DM, EPS, ALU.mult, ALU.add), s_v)
        P.wait('vector', t, force=True)
        r1 = E('vector', lambda e, srs=srs, sv1=sv1: e.reciprocal(srs, sv1), s_v)
        dep.note_w(r1, f'stat_v1{par}')
        wait('scalar', r1)
        rs = E('scalar', lambda e, srs=srs: e.activation(srs, srs, AF.Sqrt), s_s)
        dep.note_w(rs, f'stat_rs{par}')
        wait('vector', rs)
        dep.wr('vector', lbk)
        ln = E('vector', lambda e, mbuf=mbuf, srs=srs, lnb=lnb: e.tensor_scalar_mul(lnb[:, :], mbuf[:, :], srs), s_v)
        dep.note_w(ln, lbk)
        dep.note_r(ln, mk, f'stat_rs{par}')
        wait('tensor', ln)
        transpose_to(fm, 128 * i, lnb, key_out, src_key=lbk)

    def ln_pass(sq, src_fn, src_key_fn, with_load, ntiles=NT):
        pend = None
        for i in range(ntiles):
            if with_load:
                if i == 0:
                    x_load(sq, 0, xslot[0][:, :], 'xs0')
                    x_load(sq, 1, xslot[1][:, :], 'xs1')
                elif i + 1 < ntiles:
                    x_load(sq, i + 1, xslot[(i + 1) % 2][:, :], f'xs{(i + 1) % 2}')
            par = i % 2
            xm, mbuf, mk = ln_A(i, src_fn(i), src_key_fn(i), par)
            vv = ln_S(i, par, xm, mbuf, mk)
            if pend is not None:
                ln_B(*pend)
            pend = (i, par, vv, mbuf, mk, f'fmc:{i // 4}')
        ln_B(*pend)

    def proj_sweep(sq, kind, wbuf, wkey):
        boff = {'q': 0, 'k': DM, 'v': 2 * DM}[kind]
        pend_tp = [None]
        for i in range(NT):
            par = i % 2
            pq = pq0 if par == 0 else pq1
            pks = (f'pq{par}lo', f'pq{par}hi')
            dep.rd('tensor', f'fmc:{i // 4}', wkey, 'biaspg')
            dep.wr('tensor', pks[0], pks[1])
            for c in range(DC):
                for oh in range(2):
                    E('tensor', lambda e, i=i, c=c, oh=oh, pq=pq, wb=wbuf:
                      e.matmul(pq[:, 512 * oh:512 * (oh + 1)],
                               fm[:, c, 128 * i:128 * (i + 1)],
                               wb[:, c, 512 * oh:512 * (oh + 1)],
                               start=(c == 0), stop=False))
            for oh in range(2):
                tk = E('tensor', lambda e, oh=oh, pq=pq, boff=boff:
                       e.matmul(pq[:, 512 * oh:512 * (oh + 1)], ones_bf[0:1, :],
                                biaspg[0:1, boff + 512 * oh:boff + 512 * (oh + 1)],
                                start=False, stop=True), s_t if oh == 1 else None)
            mm = tk
            dep.note_w(mm, pks[0], pks[1])
            dep.note_r(mm, f'fmc:{i // 4}', wkey)
            if pj_steps < 2:
                continue
            if kind in ('q', 'k'):
                cos_ap = AP(cosr, i * DH, [[NT * DH, 128], [0, NH], [1, DH]])
                sin_ap = AP(sinr, i * DH, [[NT * DH, 128], [0, NH], [1, DH]])
                wait('scalar', mm)
                dep.wr('scalar', 'qsbb')
                qv = E('scalar', lambda e, pq=pq: e.copy(qsbb[:, :], pq[:, :]), s_s)
                dep.note_w(qv, 'qsbb')
                dep.note_r(qv, *pks)
                wait('vector', qv)
                dep.rd('vector', 'tables')
                dep.wr('vector', 'msc1b')
                t1 = E('vector', lambda e, ca=cos_ap: e.tensor_tensor(
                    msc1b[:, :].rearrange('p (h d) -> p h d', d=DH),
                    qsbb[:, :].rearrange('p (h d) -> p h d', d=DH), ca, ALU.mult), s_v)
                dep.note_w(t1, 'msc1b')
                dep.wr('scalar', 'msc2b')
                P.wait('scalar', qv, force=True)
                E('scalar', lambda e: e.activation(
                    AP(msc2b, 0, [[DM, 128], [DH, NH], [1, 32]]),
                    AP(qsbb, 32, [[DM, 128], [DH, NH], [1, 32]]), AF.Copy, scale=-1.0))
                rot = E('scalar', lambda e: e.activation(
                    AP(msc2b, 32, [[DM, 128], [DH, NH], [1, 32]]),
                    AP(qsbb, 0, [[DM, 128], [DH, NH], [1, 32]]), AF.Copy), s_s)
                dep.note_w(rot, 'msc2b')
                dep.note_r(rot, 'qsbb')
                dep.note_r(t1, 'qsbb')
                wait('vector', rot)
                sm = E('vector', lambda e, sa=sin_ap: e.tensor_tensor(
                    msc2b[:, :].rearrange('p (h d) -> p h d', d=DH),
                    msc2b[:, :].rearrange('p (h d) -> p h d', d=DH), sa, ALU.mult), s_v)
                qbuf, qbk = (lnbf, 'lnbf') if i % 2 == 0 else (lnbf2, 'lnbf2')
                dep.wr('vector', qbk)
                P.wait('vector', t1, force=True)
                P.wait('vector', sm, force=True)
                qr = E('vector', lambda e, qbuf=qbuf: e.tensor_tensor(qbuf[:, :], msc1b[:, :], msc2b[:, :], ALU.add), s_v)
                dep.note_w(qr, qbk)
                dep.note_r(qr, 'msc1b', 'msc2b')
                if pj_steps < 3:
                    continue
                acc = qn2m if kind == 'q' else kn2m
                acck = 'qn2m' if kind == 'q' else 'kn2m'
                dep.wr('scalar', 'qsbb')
                P.wait('scalar', qv, force=True)
                P.wait('scalar', rot, force=True)
                wait('scalar', t1)
                sq = E('scalar', lambda e: e.activation(qsbb[:, :], qsbb[:, :], AF.Square), s_s)
                dep.note_w(sq, 'qsbb')
                dep.wr('vector', 'qn2', acck)
                wait('vector', sq)
                t = E('vector', lambda e: e.reduce_sum(
                    qn2[:, :], qsbb[:, :].rearrange('p (h d) -> p h d', d=DH), axis=AX.X), s_v)
                dep.note_r(t, 'qsbb')
                if i == 0:
                    E('vector', lambda e, acc=acc: e.memset(acc[:, :], 0.0))
                P.wait('vector', t, force=True)
                t = E('vector', lambda e: e.reduce_max(stat_v1[:, 0:1], qn2[:, :], axis=AX.X), s_v)
                P.wait('vector', t, force=True)
                n2 = E('vector', lambda e, acc=acc: e.tensor_tensor(acc[:, :], acc[:, :], stat_v1[:, 0:1], ALU.max), s_v)
                dep.note_w(n2, acck, 'msc1b', 'stat_v1')
                if pj_steps < 4:
                    continue
                dst = q_fm if kind == 'q' else k_fm
                dkey = 'q_fm' if kind == 'q' else 'k_fm'

                def do_tp(i=i, qr=qr, qbuf=qbuf, qbk=qbk, dst=dst, dkey=dkey):
                    wait('tensor', qr)
                    if i == 0:
                        dep.wr('tensor', dkey)
                    transpose_to(dst, 128 * i, qbuf, dkey, src_key=qbk)
                if pend_tp[0] is not None:
                    pend_tp[0]()
                pend_tp[0] = do_tp
            else:
                if pend_tp[0] is not None:
                    pend_tp[0]()
                    pend_tp[0] = None
                wait('scalar', mm)
                if i == 0:
                    dep.wr('scalar', 'v_aug')
                va = E('scalar', lambda e, i=i, pq=pq: e.activation(
                    AP(v_aug, i * (NH * 65), [[NT * NH * 65, 128], [65, NH], [1, 64]]),
                    pq[:, :].rearrange('p (h d) -> p h d', d=DH), AF.Copy), s_s)
                dep.note_w(va, 'v_aug')
                dep.note_r(va, *pks)
                wait('vector', va)
                vo = E('vector', lambda e, i=i: e.memset(
                    AP(v_aug, i * (NH * 65) + 64, [[NT * NH * 65, 128], [65, NH], [1, 1]]), 1.0), s_v)
                dep.note_w(vo, 'v_aug')
        if pend_tp[0] is not None:
            pend_tp[0]()
            pend_tp[0] = None

    def dump(name, buf, *keys):
        if not debug:
            return
        for k in keys:
            dep.rd('gpsimd', k)
        if s_dbg.n:
            wait('gpsimd', (s_dbg, s_dbg.n))
        t = s_dbg.tok(16)
        em('gpsimd', lambda e, d=ddbg[name], b=buf, h=s_dbg.h:
           e.dma_start(d.ap(), b).then_inc(h, 16))
        for k in keys:
            dep.note_r(t, k)

    _stopped = [False]

    def phase_gate(name):
        if _stopped[0]:
            return False
        if stop_after is not None and name == stop_after:
            _stopped[0] = True
        return True

    fill(wbufA, 'wbufA', qkv_slices(swq), s_wa)
    for sq in range(nseq):
        if not phase_gate('prelude'):
            break
        if phase_gate('ln1'):
            ln_pass(sq, lambda i: xslot[i % 2][:, :], lambda i: (f'xs{i % 2}',), True, ntiles=ln1_tiles)
        if sq == 0:
            dump('dbg_fm1', fm[:, :, :], *[f'fmc:{c}' for c in range(4)])
        if phase_gate('q'):
            proj_sweep(sq, 'q', wbufA, 'wbufA')
            fill(wbufB, 'wbufB', qkv_slices(swk), s_wb)
        if phase_gate('k'):
            proj_sweep(sq, 'k', wbufB, 'wbufB')
            fill(wbufA, 'wbufA', qkv_slices(swv), s_wa)
        if phase_gate('v'):
            proj_sweep(sq, 'v', wbufA, 'wbufA')
            fill(wbufB, 'wbufB', qkv_slices(swo), s_wb)

        if sq == 0:
            dump('dbg_q', q_fm[:, :, :], 'q_fm')
            dump('dbg_k', k_fm[:, :, :], 'k_fm')
            dump('dbg_v', v_aug[:, :, :], 'v_aug')
        if not phase_gate('negc'):
            continue
        dep.rd('tensor', 'qn2m', 'kn2m', 'tables')
        dep.wr('tensor', 'pt0')
        E('tensor', lambda e: e.transpose(pt[0:1, 0:128], qn2m[:, :], ident_bf[:, :]))
        tC = E('tensor', lambda e: e.transpose(pt[0:1, 128:256], kn2m[:, :], ident_bf[:, :]), s_t)
        dep.note_w(tC, 'pt0')
        dep.note_r(tC, 'qn2m', 'kn2m')
        wait('vector', tC)
        dep.wr('vector', 'cnst')
        E('vector', lambda e: e.reduce_max(cnst[0:1, 0:1], pt[0:1, 0:128], axis=AX.X))
        t = E('vector', lambda e: e.reduce_max(cnst[0:1, 1:2], pt[0:1, 128:256], axis=AX.X), s_v)
        P.wait('vector', t, force=True)
        c3 = E('vector', lambda e: e.tensor_tensor(cnst[0:1, 2:3], cnst[0:1, 0:1], cnst[0:1, 1:2], ALU.mult), s_v)
        dep.note_r(c3, 'pt0')
        wait('scalar', c3)
        c4 = E('scalar', lambda e: e.activation(cnst[0:1, 3:4], cnst[0:1, 2:3], AF.Sqrt, scale=1.0 / 64.0), s_s)
        wait('vector', c4)
        c5 = E('vector', lambda e: e.tensor_scalar_mul(cnst[0:1, 4:5], cnst[0:1, 3:4], -1.0), s_v)
        dep.note_w(c5, 'cnst')
        wait('tensor', c5)
        dep.rd('tensor', 'ones')
        dep.wr('tensor', 'prb')
        E('tensor', lambda e: e.matmul(prb[0:64, 0:1], ones32[0:1, :], cnst[0:1, 4:5], start=True, stop=True))
        c6 = E('tensor', lambda e: e.matmul(prb[64:128, 1:2], ones32[0:1, :], cnst[0:1, 4:5], start=True, stop=True), s_t)
        dep.note_w(c6, 'prb')
        dep.note_r(c6, 'cnst', 'ones')
        wait('scalar', c6)
        dep.wr('scalar', 'negC8')
        E('scalar', lambda e: e.copy(negC8[0:64, :], prb[0:64, 0:1]))
        c7 = E('scalar', lambda e: e.copy(negC8[64:128, :], prb[64:128, 1:2]), s_s)
        dep.note_w(c7, 'negC8')
        dep.note_r(c7, 'prb')

        if sq == 0:
            dump('dbg_c', negC8[:, :], 'negC8')
        if not phase_gate('attn'):
            continue
        pcs = [(p, c) for p in range(NPAIR) for c in range(QC)]

        def av_banks(k):
            if k % 2 == 0:
                return pv, 'pv', pvo, 'pvo'
            return pq1[:, 0:512], 'pq1lo', pq1[:, 512:1024], 'pq1hi'

        def jloop(k):
            p, c = pcs[k]
            avA, avAk, avB, avBk = av_banks(k)
            jmax = 4 * c + 3
            info = {}

            def sc_mm(j):
                o = max(0, 128 * j - 512 * c)
                wdt = 512 - o
                for half in range(2):
                    h0 = 64 * half
                    pk = 'pq0lo' if half == 0 else 'pq0hi'
                    scoff = 512 * half
                    dep.rd('tensor', 'q_fm', 'k_fm')
                    dep.wr('tensor', pk)
                    sct = E('tensor', lambda e, p=p, c=c, j=j, h0=h0, o=o, wdt=wdt, scoff=scoff:
                            e.matmul(pq0[:, scoff + o:scoff + o + wdt],
                                     k_fm[h0:h0 + 64, p, 128 * j:128 * (j + 1)],
                                     q_fm[h0:h0 + 64, p, 512 * c + o:512 * c + o + wdt],
                                     start=True, stop=True), s_t)
                    dep.note_w(sct, pk)
                    info[(j, half)] = [sct, o, wdt]

            def exp_j(j):
                for half in range(2):
                    sct, o, wdt = info[(j, half)]
                    pk = 'pq0lo' if half == 0 else 'pq0hi'
                    scoff = 512 * half
                    eb = expt[2 * half + (j % 2)]
                    ebk = f'expt{2 * half + (j % 2)}'
                    wait('scalar', sct)
                    P.wait('scalar', c7, force=True)
                    dep.rd('scalar', 'negC8')
                    dep.wr('scalar', ebk)
                    ex = E('scalar', lambda e, eb=eb, scoff=scoff, o=o, wdt=wdt:
                           e.activation(eb[:, o:o + wdt], pq0[:, scoff + o:scoff + o + wdt],
                                        AF.Exp, bias=negC8[:, 0:1], scale=0.125), s_s)
                    dep.note_w(ex, ebk)
                    dep.note_r(ex, pk, 'negC8')
                    if j >= 4 * c:
                        wait('gpsimd', ex)
                        dep.rd('gpsimd', 'tables')
                        dep.wr('gpsimd', ebk)
                        ex = E('gpsimd', lambda e, eb=eb, o=o: e.tensor_tensor(
                            eb[:, o:o + 128], eb[:, o:o + 128], causal[:, :], ALU.mult), s_g)
                        dep.note_w(ex, ebk)
                    info[(j, half)] = (ex, eb, ebk, o, wdt)

            def av(j):
                for half in range(2):
                    ex, eb, ebk, o, wdt = info[(j, half)]
                    ebs = eb[:, o:o + wdt]
                    hh = 2 * p + half
                    wait('tensor', ex)
                    dep.rd('tensor', 'v_aug')
                    dep.wr('tensor', avAk if half == 0 else avBk)
                    if half == 0:
                        tk = E('tensor', lambda e, j=j, hh=hh, ebs=ebs, o=o, wdt=wdt, st=(j == 0), sp=(j == jmax), avA=avA:
                               e.matmul(avA[0:65, o:o + wdt], v_aug[:, j, 65 * hh:65 * hh + 65],
                                        ebs, start=st, stop=sp), s_t)
                    else:
                        E('tensor', lambda e, j=j, ebs=ebs, o=o, wdt=wdt, st=(j == 0), sp=(j == jmax), avB=avB:
                          e.matmul(avB[0:1, o:o + wdt], ones_col[:, 0:1],
                                   ebs, start=st, stop=sp))
                        tk = E('tensor', lambda e, j=j, hh=hh, ebs=ebs, o=o, wdt=wdt, st=(j == 0), sp=(j == jmax), avB=avB:
                               e.matmul(avB[64:128, o:o + wdt], v_aug[:, j, 65 * hh:65 * hh + 64],
                                        ebs, start=st, stop=sp), s_t)
                    dep.note_w(tk, avAk if half == 0 else avBk)
                    dep.note_r(tk, ebk)

            for j in range(jmax + 1):
                sc_mm(j)
                exp_j(j)
                if j >= 1:
                    av(j - 1)
            av(jmax)
            av_done = (s_t, s_t.n)
            ra, rak = (rbA, 'rbA') if k % 2 == 0 else (rbA1, 'rbA1')
            rb_, rbk = (rbB, 'rbB') if k % 2 == 0 else (rbB1, 'rbB1')
            wait('vector', av_done)
            dep.wr('vector', rak, rbk)
            E('vector', lambda e, avA=avA, ra=ra: e.reciprocal(ra[64:65, :], avA[64:65, :]))
            rc = E('vector', lambda e, avB=avB, rb_=rb_: e.reciprocal(rb_[0:1, :], avB[0:1, :]), s_v)
            dep.note_w(rc, rak, rbk)
            dep.note_r(rc, avAk, avBk)
            return (p, c, avA, avAk, avB, avBk, rc, ra, rak, rb_, rbk)

        def norm(st8):
            p, c, avA, avAk, avB, avBk, rc, ra, rak, rb_, rbk = st8
            wait('tensor', rc)
            dep.rd('tensor', 'ones')
            dep.wr('tensor', 'prb')
            E('tensor', lambda e, ra=ra: e.matmul(prb[0:64, :], ones32[64:65, :], ra[64:65, :], start=True, stop=True))
            rb = E('tensor', lambda e, rb_=rb_: e.matmul(prb[64:128, :], ones32[0:1, :], rb_[0:1, :], start=True, stop=True), s_t)
            dep.note_w(rb, 'prb')
            dep.note_r(rb, rak, rbk)
            wait('scalar', rb)
            dep.wr('scalar', rak, rbk)
            E('scalar', lambda e, ra=ra: e.copy(ra[0:64, :], prb[0:64, :]))
            rbe = E('scalar', lambda e, rb_=rb_: e.copy(rb_[64:128, :], prb[64:128, :]), s_s)
            dep.note_w(rbe, rak, rbk)
            dep.note_r(rbe, 'prb')
            wait('vector', rbe)
            dep.wr('vector', f'fmc:{c}')
            E('vector', lambda e, p=p, c=c, avA=avA, ra=ra: e.tensor_tensor(
                fm[0:64, p, 512 * c:512 * (c + 1)], avA[0:64, :], ra[0:64, :], ALU.mult))
            fme = E('vector', lambda e, p=p, c=c, avB=avB, rb_=rb_: e.tensor_tensor(
                fm[64:128, p, 512 * c:512 * (c + 1)], avB[64:128, :], rb_[64:128, :], ALU.mult), s_v)
            dep.note_w(fme, f'fmc:{c}')
            dep.note_r(fme, avAk, avBk, rak, rbk)

        pend = None
        for k in range(len(pcs)):
            st8 = jloop(k)
            if pend is not None:
                norm(pend)
            pend = st8
        norm(pend)
        phase2_done = (s_t, s_t.n)
        dep.note_r(phase2_done, 'v_aug', 'q_fm', 'k_fm')
        dep.note_r((s_v, s_v.n), 'q_fm', 'k_fm')

        if sq == 0:
            dump('dbg_attn', fm[:, :, :], *[f'fmc:{c}' for c in range(4)])
        if not phase_gate('wo'):
            continue
        phase_gate('ln2')
        phase_gate('gate')

        def gate_mms(i):
            sl = 8 * (i % 4)
            dep.rd('tensor', f'fmc:{i // 4}', 'tables', 'gateT')
            dep.wr('tensor', 'prb')
            for c in range(DC):
                tk = E('tensor', lambda e, i=i, c=c, sl=sl:
                       e.matmul(prb[:, sl:sl + 8], fm[:, c, 128 * i:128 * (i + 1)],
                                gateT[:, c, 0:8], start=(c == 0), stop=(c == DC - 1)),
                       s_t if c == DC - 1 else None)
            dep.note_w(tk, 'prb')
            dep.note_r(tk, f'fmc:{i // 4}')
            return tk

        def gate_chain(g, tk):
            L = prb[:, 0:32].rearrange('p (a b) -> p a b', b=8)
            mrep = AP(g_m, 0, [[4, 128], [1, 4], [0, 8]])
            rrep = AP(g_r, 0, [[4, 128], [1, 4], [0, 8]])
            a3 = g_s[:, :].rearrange('p (a b) -> p a b', b=8)
            b3 = g_t1[:, :].rearrange('p (a b) -> p a b', b=8)
            wait('vector', tk)
            dep.wr('vector', 'g_m', 'g_s', 'g_t1', 'g_r')
            ch = []

            def step(eng, fn):
                for tt in ch[-1:]:
                    P.wait(eng, tt, force=True)
                ch.append(E(eng, fn, s_v if eng == 'vector' else s_s))
            step('vector', lambda e: e.reduce_max(g_m[:, :], L, axis=AX.X, negate=True))
            step('vector', lambda e: e.tensor_tensor(a3, L, mrep, ALU.add))
            step('scalar', lambda e: e.activation(g_t1[:, :], g_s[:, :], AF.Exp))
            step('vector', lambda e: e.reduce_sum(g_r[:, :], b3, axis=AX.X))
            step('vector', lambda e: e.reciprocal(g_r[:, :], g_r[:, :]))
            step('vector', lambda e: e.tensor_tensor(a3, b3, rrep, ALU.mult))
            step('vector', lambda e: e.reduce_max(g_m[:, :], a3, axis=AX.X))
            step('vector', lambda e: e.tensor_tensor(b3, a3, mrep, ALU.is_ge))
            step('vector', lambda e: e.tensor_scalar(g_t1[:, :], g_t1[:, :], -1.0, 1.0, ALU.mult, ALU.add))
            step('vector', lambda e: e.tensor_tensor(b3, a3, b3, ALU.mult))
            step('vector', lambda e: e.reduce_max(g_m[:, :], b3, axis=AX.X))
            step('vector', lambda e: e.tensor_tensor(b3, a3, mrep, ALU.is_ge))
            ks = [f'w_tok:{4 * g + t}' for t in range(4)]
            dep.wr('vector', *ks)
            step('vector', lambda e, g=g: e.tensor_tensor(
                w_tok[:, 4 * g:4 * (g + 1), :], a3, b3, ALU.mult))
            wt = ch[-1]
            dep.note_w(wt, *ks, 'g_s', 'g_m', 'g_t1', 'g_r')
            dep.note_r(wt, 'prb')

        pend_ln = None
        for i in range(NT):
            par = i % 2
            xr = x_load(sq, i, res[:, i, :], f'res:{i}')
            pq = pq0 if par == 0 else pq1
            pks = (f'pq{par}lo', f'pq{par}hi')
            dep.rd('tensor', 'wbufB', 'biaspg', f'fmc:{i // 4}')
            dep.wr('tensor', pks[0], pks[1])
            for p in range(NPAIR):
                for oh in range(2):
                    E('tensor', lambda e, i=i, p=p, oh=oh, pq=pq:
                      e.matmul(pq[:, 512 * oh:512 * (oh + 1)],
                               fm[:, p, 128 * i:128 * (i + 1)],
                               wbufB[:, p, 512 * oh:512 * (oh + 1)],
                               start=(p == 0), stop=False))
            for oh in range(2):
                tk = E('tensor', lambda e, oh=oh, pq=pq:
                       e.matmul(pq[:, 512 * oh:512 * (oh + 1)], ones_bf[0:1, :],
                                biaspg[0:1, 3 * DM + 512 * oh:3 * DM + 512 * (oh + 1)],
                                start=False, stop=True), s_t if oh == 1 else None)
            dep.note_w(tk, pks[0], pks[1])
            dep.note_r(tk, 'wbufB', f'fmc:{i // 4}')
            wait('vector', tk)
            wait('vector', xr)
            dep.wr('vector', f'res:{i}')
            E('vector', lambda e, i=i, pq=pq: e.tensor_tensor(
                res[:, i, 0:512], pq[:, 0:512], res[:, i, 0:512], ALU.add))
            ra = E('vector', lambda e, i=i, pq=pq: e.tensor_tensor(
                res[:, i, 512:1024], pq[:, 512:1024], res[:, i, 512:1024], ALU.add), s_v)
            dep.note_w(ra, f'res:{i}')
            dep.note_r(ra, pks[0], pks[1])
            xm, mbuf, mk = ln_A(i, res[:, i, :], (f'res:{i}',), par)
            vv = ln_S(i, par, xm, mbuf, mk)
            if pend_ln is not None:
                pi = pend_ln[0]
                ln_B(*pend_ln)
                gtk = gate_mms(pi)
                if pi % 4 == 3:
                    gate_chain(pi // 4, gtk)
            pend_ln = (i, par, vv, mbuf, mk, f'fmc:{i // 4}')
        ln_B(*pend_ln)
        gtk = gate_mms(NT - 1)
        gate_chain(NT // 4 - 1, gtk)

        for b in range(2):
            dep.rd('tensor', 'tables')
            dep.wr('tensor', 'pt0', 'pt1')
            for t8 in range(8):
                i = 8 * b + t8
                dep.rd('tensor', f'w_tok:{i}')
                tk = E('tensor', lambda e, i=i, t8=t8: e.transpose(
                    pt[0:8, 128 * t8:128 * (t8 + 1)], w_tok[:, i, :], ident_bf[:, :]),
                    s_t if t8 == 7 else None)
            dep.note_w(tk, 'pt0', 'pt1')
            for t8 in range(8):
                dep.note_r(tk, f'w_tok:{8 * b + t8}')
            wait('vector', tk)
            dep.wr('vector', 'w_fm')
            wf = E('vector', lambda e, b=b: e.tensor_copy(
                w_fm[0:8, 1024 * b:1024 * (b + 1)], pt[0:8, 0:1024]), s_v)
            dep.note_r(wf, 'pt0', 'pt1')
            dep.note_w(wf, 'w_fm')

        if sq == 0:
            dump('dbg_w', w_tok[:, :, :], *[f'w_tok:{i}' for i in range(NT)])
            dump('dbg_fm2', fm[:, :, :], *[f'fmc:{c}' for c in range(4)])
        if not phase_gate('ffn'):
            continue

        def wrep_for(e_n):
            dep.rd('tensor', 'w_fm', 'tables')
            WRB = [(pv[:, 0:512], 'pv'), (pvo[:, 0:512], 'pvo'),
                   (pq1[:, 0:512], 'pq1lo'), (pq1[:, 512:1024], 'pq1hi')]
            for cc in range(4):
                dep.wr('tensor', WRB[cc][1])
            for cc in range(4):
                bc = E('tensor', lambda e, e_n=e_n, cc=cc, pz=WRB[cc][0]:
                       e.matmul(pz[:, :], sel[:, 128 * e_n:128 * (e_n + 1)],
                                w_fm[0:8, 512 * cc:512 * (cc + 1)], start=True, stop=True),
                       s_t if cc == 3 else None)
            for cc in range(4):
                dep.note_w(bc, WRB[cc][1])
            dep.note_r(bc, 'w_fm')
            wait('vector', bc)
            dep.wr('vector', 'wrep')
            for cc in range(4):
                wr_t = E('vector', lambda e, cc=cc, pz=WRB[cc][0]:
                         e.tensor_copy(wrep[:, 512 * cc:512 * (cc + 1)], pz[:, :]),
                         s_v if cc == 3 else None)
            dep.note_w(wr_t, 'wrep')
            for cc in range(4):
                dep.note_r(wr_t, WRB[cc][1])

        wrep_for(0)
        fill(w1bufA, 'w1bufA', w1_slices(0, 1), s_wa)
        fill(w1bufB, 'w1bufB', w1_slices(0, 0), s_wb)
        fill(w2bufA, 'w2bufA', w2_slices(0), s_w2a)
        for e_i in range(NEXP):
            routed = e_i < 8
            if e_i % 2 == 0:
                w1g_buf, w1g_key, s_g = w1bufA, 'w1bufA', s_wa
                w1y_buf, w1y_key, s_y = w1bufB, 'w1bufB', s_wb
                w2_buf, w2_key = w2bufA, 'w2bufA'
                w2n_buf, w2n_key, s_w2n = w2bufB, 'w2bufB', s_w2b
            else:
                w1g_buf, w1g_key, s_g = w1bufB, 'w1bufB', s_wb
                w1y_buf, w1y_key, s_y = w1bufA, 'w1bufA', s_wa
                w2_buf, w2_key = w2bufB, 'w2bufB'
                w2n_buf, w2n_key, s_w2n = w2bufA, 'w2bufA', s_w2a
            if debug and sq == 0 and e_i == 1:
                dump('dbg_w1A', w1bufA[:, :, :], 'w1bufA')
                dump('dbg_w1B', w1bufB[:, :, :], 'w1bufB')
                dump('dbg_w2', w2_buf[:, :, :], w2_key)
            W1BANKS = [(pq0, 0, 'pq0lo'), (pq0, 512, 'pq0hi'), (pq1, 0, 'pq1lo'), (pq1, 512, 'pq1hi')]
            for j in range(NJ):
                si_last = None
                for htc in range(2):
                    tcs = (2 * htc, 2 * htc + 1)
                    dep.rd('tensor', w1g_key, f'fmc:{tcs[0]}', f'fmc:{tcs[1]}')
                    for tc in tcs:
                        dep.wr('tensor', W1BANKS[tc][2])
                    for cch in range(DC):
                        for tc in tcs:
                            psc, sl, pk = W1BANKS[tc]
                            tk = E('tensor', lambda e, j=j, tc=tc, cch=cch, psc=psc, sl=sl, wb=w1g_buf:
                                   e.matmul(psc[:, sl:sl + 512], wb[:, cch, 128 * j:128 * (j + 1)],
                                            fm[:, cch, 512 * tc:512 * (tc + 1)],
                                            start=(cch == 0), stop=(cch == DC - 1)),
                                   s_t if (cch == DC - 1 and tc == tcs[1]) else None)
                    for tc in tcs:
                        psc, sl, pk = W1BANKS[tc]
                        dep.note_w(tk, pk)
                        dep.note_r(tk, w1g_key, f'fmc:{tc}')
                    wait('scalar', tk)
                    for tc in tcs:
                        psc, sl, pk = W1BANKS[tc]
                        dep.wr('scalar', f'act:{j}')
                        si = E('scalar', lambda e, j=j, tc=tc, psc=psc, sl=sl:
                               e.activation(act[:, j, 512 * tc:512 * (tc + 1)], psc[:, sl:sl + 512], AF.Silu), s_s)
                        dep.note_w(si, f'act:{j}')
                        dep.note_r(si, pk)
                        si_last = si
                if routed:
                    wait('vector', si_last)
                    dep.rd('vector', 'wrep')
                    dep.wr('vector', f'act:{j}')
                    aw = E('vector', lambda e, j=j: e.tensor_tensor(
                        act[:, j, :], act[:, j, :], wrep[:, :], ALU.mult), s_v)
                    dep.note_w(aw, f'act:{j}')
                    dep.note_r(aw, 'wrep')
            if e_i + 1 < NEXP:
                fill(w1g_buf, w1g_key, w1_slices(e_i + 1, 0), s_g)
            for j in range(NJ):
                for htc in range(2):
                    tcs = (2 * htc, 2 * htc + 1)
                    dep.rd('tensor', w1y_key, f'fmc:{tcs[0]}', f'fmc:{tcs[1]}')
                    for tc in tcs:
                        dep.wr('tensor', W1BANKS[tc][2])
                    for cch in range(DC):
                        for tc in tcs:
                            psc, sl, pk = W1BANKS[tc]
                            tk = E('tensor', lambda e, j=j, tc=tc, cch=cch, psc=psc, sl=sl, wb=w1y_buf:
                                   e.matmul(psc[:, sl:sl + 512], wb[:, cch, 128 * j:128 * (j + 1)],
                                            fm[:, cch, 512 * tc:512 * (tc + 1)],
                                            start=(cch == 0), stop=(cch == DC - 1)),
                                   s_t if (cch == DC - 1 and tc == tcs[1]) else None)
                    for tc in tcs:
                        psc, sl, pk = W1BANKS[tc]
                        dep.note_w(tk, pk)
                        dep.note_r(tk, w1y_key, f'fmc:{tc}')
                    wait('vector', tk)
                    for tc in tcs:
                        psc, sl, pk = W1BANKS[tc]
                        dep.wr('vector', f'act:{j}')
                        ya = E('vector', lambda e, j=j, tc=tc, psc=psc, sl=sl:
                               e.tensor_tensor(act[:, j, 512 * tc:512 * (tc + 1)], psc[:, sl:sl + 512],
                                               act[:, j, 512 * tc:512 * (tc + 1)], ALU.mult), s_v)
                        dep.note_w(ya, f'act:{j}')
                        dep.note_r(ya, pk)
            if debug and sq == 0 and e_i == 1:
                dump('dbg_act0', act[:, :, :], *[f'act:{j}' for j in range(NJ)])
            if e_i + 1 < NEXP:
                fill(w1y_buf, w1y_key, w1_slices(e_i + 1, 1), s_y)
                fill(w2n_buf, w2n_key, w2_slices(e_i + 1), s_w2a if w2n_key == 'w2bufA' else s_w2b)
                if e_i + 1 < 8:
                    wrep_for(e_i + 1)
            for i in range(NT):
                if i % 2 == 0:
                    zb = [(pv[:, 0:512], 'pv'), (pvo[:, 0:512], 'pvo')]
                else:
                    zb = [(pq1[:, 0:512], 'pq1lo'), (pq1[:, 512:1024], 'pq1hi')]
                dep.rd('tensor', w2_key, *[f'act:{j}' for j in range(NJ)])
                dep.wr('tensor', zb[0][1], zb[1][1])
                for j in range(NJ):
                    for oh in range(2):
                        tk = E('tensor', lambda e, i=i, oh=oh, j=j, pz=zb[oh][0], wb=w2_buf:
                               e.matmul(pz[:, :], act[:, j, 128 * i:128 * (i + 1)],
                                        wb[:, j, 512 * oh:512 * (oh + 1)],
                                        start=(j == 0), stop=(j == NJ - 1)),
                               s_t if (j == NJ - 1 and oh == 1) else None)
                dep.note_w(tk, zb[0][1], zb[1][1])
                dep.note_r(tk, w2_key, *[f'act:{j}' for j in range(NJ)])
                wait('vector', tk)
                dep.wr('vector', f'res:{i}')
                for oh in range(2):
                    za = E('vector', lambda e, i=i, oh=oh, pz=zb[oh][0]: e.tensor_tensor(
                        res[:, i, 512 * oh:512 * (oh + 1)], pz[:, :],
                        res[:, i, 512 * oh:512 * (oh + 1)], ALU.add), s_v if oh == 1 else None)
                za = (s_v, s_v.n)
                dep.note_w(za, f'res:{i}')
                dep.note_r(za, zb[0][1], zb[1][1])
                if e_i == NEXP - 1:
                    wait('gpsimd', za)
                    s_out.tok(16)
                    em('gpsimd', lambda e, sq=sq, i=i, h=s_out.h:
                       e.dma_start(dout[sq, 128 * i:128 * (i + 1), :], res[:, i, :]).then_inc(h, 16))
        out_tot = (s_out, s_out.n)
        for i in range(NT):
            dep.note_r(out_tot, f'res:{i}')
        dep.note_r(out_tot, 'q_fm')
        dep.note_r(out_tot, 'k_fm')
        if sq + 1 < nseq:
            fill(wbufA, 'wbufA', qkv_slices(swq), s_wa)

    if s_out.n == 0:
        dep.wr('vector', 'xs0')
        mz = E('vector', lambda e: e.memset(xslot[0][:, :], 0.0), s_v)
        dep.note_w(mz, 'xs0')
        wait('gpsimd', mz)
        s_out.tok(16)
        em('gpsimd', lambda e, h=s_out.h: e.dma_start(dout[0, 0:128, :], xslot[0][:, :]).then_inc(h, 16))
    em('gpsimd', lambda e, h=s_out.h, v=s_out.n: e.wait_ge(h, v))
    if debug and s_dbg.n:
        em('gpsimd', lambda e, h=s_dbg.h, v=s_dbg.n: e.wait_ge(h, v))

    with nc.Block() as block:
        for name in ENGS:
            lst = P.ops[name]
            getattr(block, name)(lambda eng, lst=lst: [f(eng) for f in lst])

    stack.close()
    nc.finalize()
    return nc


_CACHE = {}


def _get_nc(nseq=NSEQ):
    if nseq not in _CACHE:
        _CACHE[nseq] = build_program(nseq)
    return _CACHE[nseq]


def make_tables():
    dh = DH
    inv_freq = (1.0 / (np.float32(10000.0) ** (np.arange(0, dh, 2, dtype=np.float32) / np.float32(dh)))).astype(np.float32)
    t = np.arange(S, dtype=np.float32)
    freqs = (t[:, None] * inv_freq[None, :]).astype(np.float32)
    emb = np.concatenate([freqs, freqs], axis=-1)
    return dict(
        cos_t=np.cos(emb).astype(np.float32),
        sin_t=np.sin(emb).astype(np.float32),
        ident_bf=np.eye(128, dtype=ml_dtypes.bfloat16),
        causal_bf=np.triu(np.ones((128, 128), dtype=np.float32)).astype(ml_dtypes.bfloat16),
        sel_bf=_sel(),
    )


def _sel():
    sel = np.zeros((8, 1024), dtype=ml_dtypes.bfloat16)
    for e in range(8):
        sel[e, 128 * e:128 * (e + 1)] = 1.0
    return sel


def make_in_map(x_shard, W):
    m = dict(W)
    m['x'] = x_shard
    return m


def kernel(L0_attn_x, L0_attn_Wq, L0_attn_bq, L0_attn_Wk, L0_attn_bk,
           L0_attn_Wv, L0_attn_bv, L0_attn_Wo, L0_attn_bo,
           L0_ffn_gate_w, L0_ffn_W1, L0_ffn_W2, L0_ffn_Ws1, L0_ffn_Ws2):
    x = np.ascontiguousarray(np.asarray(L0_attn_x, dtype=np.float32))
    per = x.shape[0] // NCORES
    assert per == NSEQ
    nc = _get_nc(NSEQ)
    W = dict(
        wq=np.ascontiguousarray(L0_attn_Wq, dtype=np.float32),
        wk=np.ascontiguousarray(L0_attn_Wk, dtype=np.float32),
        wv=np.ascontiguousarray(L0_attn_Wv, dtype=np.float32),
        wo=np.ascontiguousarray(L0_attn_Wo, dtype=np.float32),
        bq=np.ascontiguousarray(L0_attn_bq, dtype=np.float32),
        bk=np.ascontiguousarray(L0_attn_bk, dtype=np.float32),
        bv=np.ascontiguousarray(L0_attn_bv, dtype=np.float32),
        bo=np.ascontiguousarray(L0_attn_bo, dtype=np.float32),
        gate_w=np.ascontiguousarray(L0_ffn_gate_w, dtype=np.float32),
        w1=np.ascontiguousarray(L0_ffn_W1, dtype=np.float32),
        w2=np.ascontiguousarray(L0_ffn_W2, dtype=np.float32),
        ws1=np.ascontiguousarray(L0_ffn_Ws1, dtype=np.float32),
        ws2=np.ascontiguousarray(L0_ffn_Ws2, dtype=np.float32),
        **make_tables(),
    )
    in_maps = [make_in_map(x[per * i:per * (i + 1)], W) for i in range(NCORES)]
    resu = run_bass_kernel_spmd(nc, in_maps, core_ids=list(range(NCORES)))
    return np.concatenate([np.asarray(r['out']) for r in resu.results], axis=0)
